# version 12
# baseline (speedup 1.0000x reference)
import numpy as np
import ml_dtypes
import concourse.bass as bass
import concourse.mybir as mybir
from concourse.bass_utils import run_bass_kernel_spmd

F32 = mybir.dt.float32
BF16 = mybir.dt.bfloat16
ALU = mybir.AluOpType
ACT = mybir.ActivationFunctionType

C = 2048
NKC = 16
TOK = 2048
OWN = 1024
TBW = 512
NTB = OWN // TBW
D = 32
E = 33
NGRP = 16
HID = 8192
NHC = HID // 128
HS = 1024
NHS = HID // HS
THS = 512
LN_EPS = 1e-5
ATT_EPS = 1e-15
RMS_EPS = float(np.finfo(np.float32).eps)
INV_C = 1.0 / C
INV_HID = 1.0 / HID


class Emit:

    def __init__(self, eng):
        self.e = eng
        self.dirty_w = set()
        self.dirty_r = set()

    def _names(self, aps):
        out = set()
        for a in aps:
            if a is not None and hasattr(a, "tensor"):
                out.add(a.tensor.name)
        return out

    def _pre(self, ins, outs):
        rd = self._names(ins)
        wr = self._names(outs)
        if (rd & self.dirty_w) or (wr & (self.dirty_w | self.dirty_r)):
            self.e.drain()
            self.dirty_w = set()
            self.dirty_r = set()
        self.dirty_r |= rd

    def _post(self, outs):
        self.dirty_w |= self._names(outs)

    def copy(self, out, in_):
        self._pre([in_], [out])
        if hasattr(self.e, "tensor_copy"):
            r = self.e.tensor_copy(out, in_)
        else:
            r = self.e.copy(out, in_)
        self._post([out])
        return r

    def ts(self, out, in_, s1, op0, s2=None, op1=None):
        self._pre([in_], [out])
        if op1 is None:
            r = self.e.tensor_scalar(out, in_, s1, None, op0)
        else:
            r = self.e.tensor_scalar(out, in_, s1, s2, op0, op1)
        self._post([out])
        return r

    def tt(self, out, a, b, op):
        self._pre([a, b], [out])
        r = self.e.tensor_tensor(out, a, b, op)
        self._post([out])
        return r

    def stt(self, out, in0, scalar, in1, op0, op1):
        self._pre([in0, in1, scalar if hasattr(scalar, "tensor") else None], [out])
        r = self.e.scalar_tensor_tensor(out, in0, scalar, in1, op0, op1)
        self._post([out])
        return r

    def act(self, out, in_, func, bias=None, scale=None):
        self._pre([in_], [out])
        kw = {}
        if bias is not None:
            kw["bias"] = bias
        if scale is not None:
            kw["scale"] = scale
        r = self.e.activation(out, in_, func, **kw)
        self._post([out])
        return r

    def recip(self, out, in_):
        self._pre([in_], [out])
        r = self.e.reciprocal(out, in_)
        self._post([out])
        return r

    def memset(self, out, val):
        self._pre([], [out])
        r = self.e.memset(out, val)
        self._post([out])
        return r


class Sem:
    def __init__(self, nc, name):
        self.cm = nc.semaphore(name)
        self.sem = self.cm.__enter__()
        self.n = 0


def build_program(debug=False):
    nc = bass.Bass("TRN2", target_bir_lowering=False)

    def sb(name, shape, dt):
        cm = nc.sbuf_tensor(name, shape, dt)
        return cm.__enter__(), cm

    def psum(name, shape, dt=F32):
        cm = nc.psum_tensor(name, shape, dt)
        return cm.__enter__(), cm

    class Ring:
        def __init__(self, name, shape, dt, k):
            self.cms = [nc.sbuf_tensor(f"{name}_{i}", shape, dt) for i in range(k)]
            self.ts = [cm.__enter__() for cm in self.cms]

        def __getitem__(self, i):
            return self.ts[i][:]

        def __exit__(self, *a):
            for cm in reversed(self.cms):
                cm.__exit__(None, None, None)

    class Pair:
        def __init__(self, name, shape, dt):
            self.cms = [nc.sbuf_tensor(f"{name}_{i}", shape, dt) for i in range(2)]
            self.ts = [cm.__enter__() for cm in self.cms]

        def __getitem__(self, i):
            return self.ts[i][:]

        def __exit__(self, *a):
            self.cms[1].__exit__(None, None, None)
            self.cms[0].__exit__(None, None, None)

    def free(*cms):
        for cm in reversed(cms):
            cm.__exit__(None, None, None)

    xT = nc.dram_tensor("xT", [C, TOK], F32, kind="ExternalInput").ap()
    wqT = nc.dram_tensor("wqT", [C, 3 * C], F32, kind="ExternalInput").ap()
    wpT = nc.dram_tensor("wpT", [C, C], F32, kind="ExternalInput").ap()
    wfn = nc.dram_tensor("wfn", [HID, C], F32, kind="ExternalInput").ap()
    w2n = nc.dram_tensor("w2n", [C, HID], F32, kind="ExternalInput").ap()
    bpd = nc.dram_tensor("bpd", [128, NKC], F32, kind="ExternalInput").ap()
    seld = nc.dram_tensor("seld", [128, 2], BF16, kind="ExternalInput").ap()
    ind2d = nc.dram_tensor("ind2d", [2, 128], BF16, kind="ExternalInput").ap()
    ind4d = nc.dram_tensor("ind4d", [4, 128], BF16, kind="ExternalInput").ap()
    ones1d = nc.dram_tensor("ones1d", [1, 128], BF16, kind="ExternalInput").ap()
    onesfd = nc.dram_tensor("onesfd", [128, 1], F32, kind="ExternalInput").ap()
    idnd = nc.dram_tensor("idnd", [128, 128], F32, kind="ExternalInput").ap()
    idnbd = nc.dram_tensor("idnbd", [128, 128], BF16, kind="ExternalInput").ap()
    epsd = nc.dram_tensor("epsd", [1, 2], F32, kind="ExternalInput").ap()
    y = nc.dram_tensor("y", [C, OWN], F32, kind="ExternalOutput").ap()
    w1s = nc.dram_tensor("w1s", [NHC, NKC, 128, 128], BF16, kind="Internal").ap()
    w2s = nc.dram_tensor("w2s", [NHC, NKC, 128, 128], BF16, kind="Internal").ap()
    dbg = {}
    if debug:
        for nm, sh, dt in [
            ("d_xnh", [128, NKC * OWN], BF16), ("d_xnl", [128, NKC * OWN], BF16),
            ("d_bd", [128, NGRP * 128], F32), ("d_ncol", [128, NGRP * 4], F32),
            ("d_q", [128, NKC * TBW], F32), ("d_atth", [128, NKC * TBW], BF16),
            ("d_ffnr", [128, NKC * TBW], F32),
        ]:
            dbg[nm] = nc.dram_tensor(nm, sh, dt, kind="ExternalOutput").ap()

    sel, _ = sb("sel", [128, 2], BF16)
    ind2, _ = sb("ind2", [2, 128], BF16)
    ind4, _ = sb("ind4", [4, 128], BF16)
    ones1, _ = sb("ones1", [1, 128], BF16)
    onesf, _ = sb("onesf", [128, 1], F32)
    idn, _ = sb("idn", [128, 128], F32)
    idnb, _ = sb("idnb", [128, 128], BF16)
    zt, _ = sb("zt", [128, 160], BF16)
    epst, _ = sb("epst", [1, 2], F32)
    bpt, _ = sb("bpt", [128, NKC], F32)
    xnh, _ = sb("xnh", [128, NKC * OWN], BF16)
    xnl, _ = sb("xnl", [128, NKC * OWN], BF16)
    bd, _ = sb("bd", [128, NGRP * 128], F32)
    ncol, _ = sb("ncol", [128, NGRP * 4], F32)
    rstd_b, _ = sb("rstd_b", [1, 2 * OWN], BF16)

    def S(name):
        return Sem(nc, name)

    s_cst = S("s_cst")
    s_wfb = [S(f"s_wfb{i}") for i in range(2)]
    s_tg = S("s_tg")
    s_tq = S("s_tq")
    s_tpm = S("s_tpm")
    s_tse = S("s_tse")
    s_tv = S("s_tv")
    s_tw = [S(f"s_tw{i}") for i in range(2)]
    s_ldx = [S("s_ldx0"), S("s_ldx1")]
    s_xnv = S("s_xnv")
    s_xnc = S("s_xnc")
    s_sq = S("s_sq")
    s_sqr = S("s_sqr")
    s_stat = S("s_stat")
    s_lnv = S("s_lnv")
    s_wkv = [S("s_wkv0"), S("s_wkv1")]
    s_wkvb = [S("s_wkvb0"), S("s_wkvb1")]
    s_wkvs = S("s_wkvs")
    s_kvmm = S("s_kvmm")
    s_kve = S("s_kve")
    s_tpmm = S("s_tpmm")
    s_tpe = S("s_tpe")
    s_vkmm = S("s_vkmm")
    s_vke = S("s_vke")
    s_wq = [S("s_wq0"), S("s_wq1")]
    s_wqb = [S("s_wqb0"), S("s_wqb1")]
    s_wqs = S("s_wqs")
    s_qmm = S("s_qmm")
    s_qe = S("s_qe")
    s_oemm = S("s_oemm")
    s_oev = S("s_oev")
    s_w1r = [S("s_w1r0"), S("s_w1r1")]
    s_fcmm = S("s_fcmm")
    s_fce = S("s_fce")
    s_gle = S("s_gle")
    s_w2r = [S("s_w2r0"), S("s_w2r1")]
    s_fpmm = S("s_fpmm")
    s_fpe = S("s_fpe")
    s_msv = S("s_msv")
    s_rsb = S("s_rsb")
    s_wp = [S("s_wp0"), S("s_wp1")]
    s_wpb = [S("s_wpb0"), S("s_wpb1")]
    s_wps = S("s_wps")
    s_pmm = S("s_pmm")
    s_fin = S("s_fin")
    s_xre = [S("s_xre0"), S("s_xre1")]
    s_out = S("s_out")
    s_you = [S("s_you0"), S("s_you1")]

    block_cm = nc.Block()
    block = block_cm.__enter__()

    @block.sync
    def _(e):
        for t, d in [(sel, seld), (ind2, ind2d), (ind4, ind4d), (ones1, ones1d),
                     (onesf, onesfd), (idn, idnd), (idnb, idnbd),
                     (bpt, bpd), (epst, epsd)]:
            e.dma_start(out=t[:], in_=d[:, :]).then_inc(s_cst.sem, 16)
    s_cst.n = 9 * 16

    @block.vector
    def _(e):
        em = Emit(e)
        e.wait_ge(s_cst.sem, s_cst.n)
        em.memset(zt[:], 0.0)
        em.memset(bd[:], 0.0)
        em.memset(ncol[:], 0.0)

    twb = Ring("twb", [128, 2048], BF16, 1); cm_twb = twb
    tred, cm_tred = sb("tred", [128, 32], F32)
    tscl, cm_tscl = sb("tscl", [128, 32], BF16)
    trec, cm_trec = sb("trec", [128, 32], F32)
    trat, cm_trat = sb("trat", [128, 2048], BF16)
    tga, cm_tga = sb("tga", [128, 2048], BF16)
    tw1n, cm_tw1n = sb("tw1n", [128, 2048], BF16)
    tw1t, cm_tw1t = sb("tw1t", [128, 16 * 128], BF16)
    ps_ta, cm_pta = psum("ps_ta", [128, 1024], BF16)
    ps_tb, cm_ptb = psum("ps_tb", [128, 1024], BF16)
    ps_tc, cm_ptc = psum("ps_tc", [128, 1024], BF16)

    tern_blocks = ([("w1", b) for b in range(HID // 128)]
                   + [("w2", rc) for rc in range(64)])
    NT_T = len(tern_blocks)

    def tern_src(blk):
        w, a = blk
        if w == "w1":
            return wfn[a * 128:(a + 1) * 128, :]
        r, c = divmod(a, 4)
        return w2n[r * 128:(r + 1) * 128, c * 2048:(c + 1) * 2048]

    def tern_dst(blk):
        w, a = blk
        if w == "w1":
            return w1s[a, :, :, :]
        r, c = divmod(a, 4)
        return w2s[16 * c:16 * (c + 1), r, :, :]

    class Tern:
        def __init__(self):
            self.ig = 0
            self.iv = 0
            self.ipe = 0
            self.isc = 0
            self.iwb = 0

        def pump_gpsimd(self, e, n):
            em = Emit(e)
            for _ in range(n):
                i = self.ig
                if i >= 1:
                    e.wait_ge(s_tv.sem, 2 * (i - 1) + 1)
                e.dma_start(out=twb[0],
                            in_=tern_src(tern_blocks[i])).then_inc(s_wfb[i % 2].sem, 16)
                self.ig += 1

        def pump_vector(self, e, em, n):
            for _ in range(n):
                i = self.iv
                b = i % 2
                e.wait_ge(s_wfb[b].sem, 16 * (i // 2 + 1))
                e.tensor_reduce(tred[:].rearrange("p (g o) -> p g o", o=1),
                                twb[0].rearrange("p (g o) -> p g o", o=64),
                                mybir.AxisListType.X, ALU.add,
                                apply_absolute_value=True)
                em.dirty_w.add("tred")
                em.ts(tscl[:], tred[:], 1.0 / 64.0, ALU.mult, 1e-8, ALU.max)
                em.recip(trec[:], tscl[:])
                rb = trec[:].rearrange("p (g o) -> p g o", o=1).broadcast_to([128, 32, 64])
                sb_ = tscl[:].rearrange("p (g o) -> p g o", o=1).broadcast_to([128, 32, 64])
                em.tt(trat[:].rearrange("p (g o) -> p g o", o=64),
                      twb[0].rearrange("p (g o) -> p g o", o=64), rb,
                      ALU.mult).then_inc(s_tv.sem, 1)
                em.ts(tga[:], trat[:], 0.5, ALU.is_gt)
                em.stt(tga[:], trat[:], -0.5, tga[:], ALU.is_ge, ALU.add)
                if i >= 1:
                    e.wait_ge(s_tpm.sem, 16 * i)
                em.stt(tw1n[:].rearrange("p (g o) -> p g o", o=64),
                       tga[:].rearrange("p (g o) -> p g o", o=64), -1.0, sb_,
                       ALU.add, ALU.mult).then_inc(s_tv.sem, 1)
                self.iv += 1

        def pump_tensor(self, e, n):
            for _ in range(n):
                i = self.ipe
                e.wait_ge(s_tv.sem, 2 * i + 2)
                for t in range(16):
                    it = 16 * i + t
                    pst = [ps_ta, ps_tb, ps_tc][it % 3]
                    if it >= 3:
                        e.wait_ge(s_tse.sem, it - 2)
                    e.transpose(pst[:, 0:128],
                                tw1n[:, t * 128:(t + 1) * 128],
                                idnb[:]).then_inc(s_tpm.sem, 1)
                self.ipe += 1

        def pump_scalar(self, e, em, n):
            for _ in range(n):
                i = self.isc
                for t in range(16):
                    it = 16 * i + t
                    e.wait_ge(s_tpm.sem, it + 1)
                    if t == 0 and i >= 1:
                        e.wait_ge(s_tw[(i - 1) % 2].sem, 16 * ((i - 1) // 2 + 1))
                    pst = [ps_ta, ps_tb, ps_tc][it % 3]
                    em.copy(tw1t[:, t * 128:(t + 1) * 128],
                            pst[:, 0:128]).then_inc(s_tse.sem, 1)
                self.isc += 1

        def pump_sync(self, e, n):
            for _ in range(n):
                i = self.iwb
                e.wait_ge(s_tse.sem, 16 * (i + 1))
                e.dma_start(out=tern_dst(tern_blocks[i]).rearrange("t p q -> p t q"),
                            in_=tw1t[:]).then_inc(s_tw[i % 2].sem, 16)
                self.iwb += 1

    tern = Tern()
    for j, s4 in enumerate(s_wfb):
        s4.n = (NT_T // 2) * 16
    s_tq.n = NT_T
    s_tv.n = NT_T * 2
    s_tpm.n = NT_T * 16
    s_tse.n = NT_T * 16
    s_tw[0].n = (NT_T // 2) * 16
    s_tw[1].n = (NT_T // 2) * 16

    @block.gpsimd
    def _(e):
        e.wait_ge(s_cst.sem, s_cst.n)
        tern.pump_gpsimd(e, 3)

    @block.vector
    def _(e):
        em = Emit(e)
        tern.pump_vector(e, em, 2)

    @block.tensor
    def _(e):
        e.wait_ge(s_cst.sem, s_cst.n)
        tern.pump_tensor(e, 1)

    @block.scalar
    def _(e):
        em = Emit(e)
        tern.pump_scalar(e, em, 1)

    xnph, cm_xnph = sb("xnph", [128, NKC * OWN], BF16)
    xnpl, cm_xnpl = sb("xnpl", [128, NKC * OWN], BF16)
    xf = Pair("xf", [128, 512], F32); cm_xf = xf
    sqb, cm_sqb = sb("sqb", [128, OWN], F32)
    lnhl, cm_lnhl = sb("lnhl", [1, 2048], BF16)
    statf, cm_statf = sb("statf", [1, 1536], F32)
    xnf = Pair("xnf", [128, 512], F32); cm_xnf = xnf

    def sqh(kc):
        return sqb[:, (kc % 2) * 512:(kc % 2) * 512 + 512]
    stat_m = statf[0:1, 0:512]
    stat_v = statf[0:1, 512:1024]
    stat_t = statf[0:1, 1024:1536]
    ps_sx, cm_psx = psum("ps_sx", [1, 512])
    ps_sxx, cm_psxx = psum("ps_sxx", [1, 512])
    ps_rb, cm_prb = psum("ps_rb", [128, 512])
    ps_mrb, cm_pmrb = psum("ps_mrb", [128, 512])

    NQ = 4

    def q_cols(qi):
        return slice(qi * 512, (qi + 1) * 512)

    @block.sync
    def _(e):
        def wait_consumers(j):
            qi, r = divmod(j, 2 * NKC)
            p, kc = divmod(r, NKC)
            if p == 0:
                e.wait_ge(s_sq.sem, qi * NKC + kc + 1)
                e.wait_ge(s_stat.sem, qi * (NKC + 1) + kc + 1)
            else:
                e.wait_ge(s_lnv.sem, qi * (2 + NKC) + 2 + kc + 1)

        j = 0
        for qi in range(NQ):
            for p in range(2):
                for kc in range(NKC):
                    if j >= 2:
                        wait_consumers(j - 2)
                    e.dma_start(out=xf[j % 2],
                                in_=xT[kc * 128:(kc + 1) * 128, q_cols(qi)]
                                ).then_inc(s_ldx[j % 2].sem, 16)
                    j += 1
            tern.pump_sync(e, 3)
    s_ldx[0].n = NQ * NKC * 16
    s_ldx[1].n = NQ * NKC * 16

    @block.scalar
    def _(e):
        em = Emit(e)
        for qi in range(NQ):
            h, qq = divmod(qi, 2)
            xdh = [xnh, xnph][h]
            c0 = qq * 512
            base = qi * 2 * NKC
            for kc in range(NKC):
                j_ = base + kc
                e.wait_ge(s_ldx[j_ % 2].sem, 16 * (j_ // 2 + 1))
                if kc >= 2:
                    e.wait_ge(s_stat.sem, qi * (NKC + 1) + kc - 1)
                elif qi >= 1:
                    e.wait_ge(s_stat.sem, (qi - 1) * (NKC + 1) + 15 + kc)
                em.act(sqh(kc), xf[j_ % 2],
                       ACT.Square).then_inc(s_sq.sem, 1)
            e.wait_ge(s_lnv.sem, qi * (2 + NKC) + 1)
            em.act(stat_t, stat_v, ACT.Sqrt,
                   bias=epst[0:1, 0:1]).then_inc(s_sqr.sem, 1)
            for kc in range(NKC):
                e.wait_ge(s_xnv.sem, qi * NKC + kc + 1)
                em.copy(xdh[:, kc * OWN + c0:kc * OWN + c0 + 512],
                        xnf[kc % 2]).then_inc(s_xnc.sem, 1)
            tern.pump_scalar(e, em, 3)
    s_sq.n = NQ * NKC
    s_sqr.n = NQ
    s_xnv.n = NQ * NKC
    s_xnc.n = NQ * NKC

    @block.tensor
    def _(e):
        for qi in range(NQ):
            for kc in range(NKC):
                e.wait_ge(s_sq.sem, qi * NKC + kc + 1)
                e.matmul(ps_sx[0:1, :], onesf[:], xf[(qi * 2 * NKC + kc) % 2],
                         start=(kc == 0), stop=(kc == NKC - 1))
                e.matmul(ps_sxx[0:1, :], onesf[:], sqh(kc),
                         start=(kc == 0),
                         stop=(kc == NKC - 1)).then_inc(s_stat.sem, 1)
            e.wait_ge(s_lnv.sem, qi * (2 + NKC) + 2)
            e.matmul(ps_rb[:], ones1[:], lnhl[0:1, 0:512],
                     start=True, stop=False)
            e.matmul(ps_rb[:], ones1[:], lnhl[0:1, 1024:1536],
                     start=False, stop=True)
            e.matmul(ps_mrb[:], ones1[:], lnhl[0:1, 512:1024],
                     start=True, stop=False)
            e.matmul(ps_mrb[:], ones1[:], lnhl[0:1, 1536:2048],
                     start=False, stop=True).then_inc(s_stat.sem, 1)
            tern.pump_tensor(e, 3)
    s_stat.n = NQ * (NKC + 1)



    @block.vector
    def _(e):
        em = Emit(e)
        for qi in range(NQ):
            h, qq = divmod(qi, 2)
            e.wait_ge(s_stat.sem, qi * (NKC + 1) + NKC)
            sm = stat_m
            sv = stat_v
            st = stat_t
            em.ts(sm, ps_sx[:], INV_C, ALU.mult)
            em.ts(sv, ps_sxx[:], INV_C, ALU.mult)
            em.tt(st, sm, sm, ALU.mult)
            em.tt(sv, sv, st, ALU.subtract).then_inc(s_lnv.sem, 1)
            e.wait_ge(s_sqr.sem, qi + 1)
            em.recip(st, st)
            em.tt(sm, sm, st, ALU.mult)
            em.copy(lnhl[0:1, 0:512], st)
            em.tt(lnhl[0:1, 1024:1536], st, lnhl[0:1, 0:512], ALU.subtract)
            em.copy(lnhl[0:1, 512:1024], sm)
            em.tt(lnhl[0:1, 1536:2048], sm, lnhl[0:1, 512:1024],
                  ALU.subtract).then_inc(s_lnv.sem, 1)
            e.wait_ge(s_stat.sem, qi * (NKC + 1) + NKC + 1)
            xdh = [xnh, xnph][h]
            xdl = [xnl, xnpl][h]
            c0 = qq * 512

            def fin_lo(kc):
                e.wait_ge(s_xnc.sem, qi * NKC + kc + 1)
                em.tt(xdl[:, kc * OWN + c0:kc * OWN + c0 + 512],
                      xnf[kc % 2],
                      xdh[:, kc * OWN + c0:kc * OWN + c0 + 512],
                      ALU.subtract).then_inc(s_lnv.sem, 1)

            for kc in range(NKC):
                j_ = (qi * 2 + 1) * NKC + kc
                e.wait_ge(s_ldx[j_ % 2].sem, 16 * (j_ // 2 + 1))
                em.tt(xnf[kc % 2], xf[j_ % 2], ps_rb[:], ALU.mult)
                em.tt(xnf[kc % 2], xnf[kc % 2], ps_mrb[:],
                      ALU.subtract).then_inc(s_xnv.sem, 1)
                if kc >= 1:
                    fin_lo(kc - 1)
            fin_lo(NKC - 1)
            tern.pump_vector(e, em, 3)
    s_lnv.n = NQ * (2 + NKC)

    @block.gpsimd
    def _(e):
        tern.pump_gpsimd(e, 4 * 3)

    if debug:
        @block.gpsimd
        def _(e):
            e.wait_ge(s_lnv.sem, NQ * (2 + NKC))
            e.dma_start(out=dbg["d_xnh"][:, :], in_=xnh[:]).then_inc(s_out.sem, 16)
            e.dma_start(out=dbg["d_xnl"][:, :], in_=xnl[:]).then_inc(s_out.sem, 16)
        s_out.n += 32

    free(cm_psx, cm_psxx, cm_prb, cm_pmrb)
    free(cm_xf, cm_sqb, cm_lnhl, cm_statf, cm_xnf)

    wkf, cm_wkf = sb("wkf", [128, 1024], F32)
    wkh = Pair("wkh", [128, NKC * 128], BF16); cm_wkh = wkh
    wkl = Pair("wkl", [128, NKC * 128], BF16); cm_wkl = wkl
    kvf = Pair("kvf", [128, 1024], F32); cm_kvf = kvf
    ktt, cm_ktt = sb("ktt", [128, NKC * 128], F32)
    vpt, cm_vpt = sb("vpt", [128, NKC * 4 * E], F32)
    ps_kv, cm_pkv = psum("ps_kv", [128, 1024])
    ps_tp0, cm_pt0 = psum("ps_tp0", [128, 512])
    ps_tp1, cm_pt1 = psum("ps_tp1", [128, 512])
    ps_vk0, cm_pv0 = psum("ps_vk0", [32, 512])

    def wk_src(grp, kc, which):
        base = C + which * C + grp * 128
        return wqT[kc * 128:(kc + 1) * 128, base:base + 128]

    @block.sync
    def _(e):
        e.wait_ge(s_lnv.sem, 4 * (2 + NKC))
        for grp in range(NGRP):
            for which in range(2):
                iw = 2 * grp + which
                base = C + which * C + grp * 128
                for hf in range(2):
                    if iw or hf:
                        e.wait_ge(s_wkvs.sem, 2 * iw + hf)
                    e.dma_start(out=wkf[:].rearrange(
                        "p (kc q) -> p kc q", kc=8),
                        in_=wqT[hf * 1024:(hf + 1) * 1024,
                                base:base + 128].rearrange(
                            "(kc p) q -> p kc q", p=128)
                        ).then_inc(s_wkv[iw % 2].sem, 16)
            tern.pump_sync(e, 5)
    s_wkv[0].n = NGRP * 32
    s_wkv[1].n = NGRP * 32

    @block.gpsimd
    def _(e):
        e.wait_ge(s_lnv.sem, 4 * (2 + NKC))
        for grp in range(NGRP):
            for which in range(2):
                iw = 2 * grp + which
                if iw >= 2:
                    e.wait_ge(s_kvmm.sem, 2 * (iw - 1))
                base = C + which * C + grp * 128
                e.dma_start(out=wkh[iw % 2][:].rearrange(
                    "p (kc q) -> p kc q", kc=NKC),
                    in_=wqT[:, base:base + 128].rearrange(
                        "(kc p) q -> p kc q", p=128)
                    ).then_inc(s_wkvb[iw % 2].sem, 16)
            tern.pump_gpsimd(e, 5)
    s_wkvb[0].n = NGRP * 16
    s_wkvb[1].n = NGRP * 16

    @block.vector
    def _(e):
        em = Emit(e)
        for grp in range(NGRP):
            for which in range(2):
                iw = 2 * grp + which
                e.wait_ge(s_wkvb[iw % 2].sem, 16 * (iw // 2 + 1))
                if iw >= 2:
                    e.wait_ge(s_kvmm.sem, 2 * (iw - 1))
                for hf in range(2):
                    e.wait_ge(s_wkv[iw % 2].sem, 16 * (2 * (iw // 2) + hf + 1))
                    em.tt(wkl[iw % 2][:, hf * 1024:(hf + 1) * 1024], wkf[:],
                          wkh[iw % 2][:, hf * 1024:(hf + 1) * 1024],
                          ALU.subtract).then_inc(s_wkvs.sem, 1)
                for tt in range(NKC):
                    e.wait_ge(s_tpmm.sem, iw * NKC + tt + 1)
                    pst = [ps_tp0, ps_tp1][tt % 2]
                    if which == 0:
                        em.copy(ktt[:, tt * 128:(tt + 1) * 128],
                                pst[:, 0:128]).then_inc(s_tpe.sem, 1)
                    else:
                        dst = vpt[:, tt * 4 * E:(tt + 1) * 4 * E].rearrange(
                            "p (j x) -> p j x", j=4)
                        em.copy(dst[:, :, 0:D],
                                pst[:, 0:128].rearrange("p (j d) -> p j d", j=4))
                        em.memset(dst[:, :, D:D + 1], 1.0).then_inc(s_tpe.sem, 1)
            e.wait_ge(s_vkmm.sem, grp + 1)
            ps = ps_vk0
            for j in range(4):
                em.copy(bd[32 * j:32 * j + 32,
                           grp * 128 + 32 * j: grp * 128 + 32 * j + 32],
                        ps[0:32, E * j:E * j + D])
                ins = em.copy(ncol[32 * j:32 * j + 32, grp * 4 + j:grp * 4 + j + 1],
                              ps[0:32, E * j + D:E * j + E])
            ins.then_inc(s_vke.sem, 1)
            tern.pump_vector(e, em, 5)
    s_wkvs.n = NGRP * 4
    s_tpe.n = NGRP * 2 * NKC
    s_vke.n = NGRP

    @block.tensor
    def _(e):
        for grp in range(NGRP):
            for which in range(2):
                iw = 2 * grp + which
                e.wait_ge(s_wkvs.sem, 2 * (iw + 1))
                for th in range(2):
                    if grp or which or th:
                        e.wait_ge(s_kve.sem, 2 * iw + th)
                    for kc in range(NKC):
                        wh = wkh[iw % 2][:, kc * 128:(kc + 1) * 128]
                        wl = wkl[iw % 2][:, kc * 128:(kc + 1) * 128]
                        first = (kc == 0)
                        last = (kc == NKC - 1)
                        for pi, (wo, xlo) in enumerate([(wh, 0), (wh, 1), (wl, 0)]):
                            xsrc = [(xnh, xnl), (xnph, xnpl)][th][xlo]
                            for s in range(2):
                                ins = e.matmul(ps_kv[:, s * 512:(s + 1) * 512], wo,
                                         xsrc[:, kc * OWN + s * 512:
                                              kc * OWN + (s + 1) * 512],
                                         start=(first and pi == 0),
                                         stop=(last and pi == 2))
                    ins.then_inc(s_kvmm.sem, 1)
                for tt in range(NKC):
                    pst = [ps_tp0, ps_tp1][tt % 2]
                    e.wait_ge(s_kve.sem, 2 * iw + (tt // 8) + 1)
                    if tt >= 2:
                        e.wait_ge(s_tpe.sem, iw * NKC + tt - 1)
                    e.transpose(pst[:, 0:128],
                                kvf[tt // 8][:, (tt % 8) * 128:(tt % 8) * 128 + 128],
                                idn[:]).then_inc(s_tpmm.sem, 1)
            ps = ps_vk0
            e.wait_ge(s_tpe.sem, (grp * 2 + 2) * NKC)
            if grp >= 1:
                e.wait_ge(s_vke.sem, grp)
            e.matmul(ps[0:32, 0:4 * E], zt[:, 0:32], zt[:, 0:4 * E],
                     start=True, stop=False)
            for tt in range(NKC):
                for j in range(4):
                    e.matmul(ps[0:32, E * j:E * j + E],
                             ktt[:, tt * 128 + 32 * j: tt * 128 + 32 * j + 32],
                             vpt[:, tt * 4 * E + j * E: tt * 4 * E + (j + 1) * E],
                             start=False, stop=False)
            e.matmul(ps[0:32, 0:4 * E], zt[:, 0:32], zt[:, 0:4 * E],
                     start=False, stop=True).then_inc(s_vkmm.sem, 1)
            tern.pump_tensor(e, 5)
    s_kvmm.n = NGRP * 4
    s_tpmm.n = NGRP * 2 * NKC
    s_vkmm.n = NGRP

    @block.scalar
    def _(e):
        em = Emit(e)
        for grp in range(NGRP):
            for which in range(2):
                iw = 2 * grp + which
                for th in range(2):
                    e.wait_ge(s_kvmm.sem, 2 * iw + th + 1)
                    if grp or which:
                        e.wait_ge(s_tpmm.sem, (iw - 1) * NKC + 8 * (th + 1))
                    if which == 0:
                        em.act(kvf[th][:, :], ps_kv[:],
                               ACT.Relu).then_inc(s_kve.sem, 1)
                    else:
                        em.copy(kvf[th][:, :], ps_kv[:]).then_inc(s_kve.sem, 1)
            tern.pump_scalar(e, em, 5)
    s_kve.n = NGRP * 4

    if debug:
        @block.gpsimd
        def _(e):
            e.wait_ge(s_vke.sem, NGRP)
            e.dma_start(out=dbg["d_bd"][:, :], in_=bd[:]).then_inc(s_out.sem, 16)
            e.dma_start(out=dbg["d_ncol"][:, :], in_=ncol[:]).then_inc(s_out.sem, 16)
        s_out.n += 32

    free(cm_pkv, cm_pt0, cm_pt1, cm_pv0)
    free(cm_xnph, cm_xnpl, cm_wkf, cm_wkh, cm_wkl, cm_kvf, cm_ktt, cm_vpt)

    for tb in range(NTB):
        t0 = tb * TBW
        prev_out_n = s_out.n
        prev_you = (s_you[0].n, s_you[1].n)

        atth, cm_ath = sb(f"atth{tb}", [128, NKC * TBW], BF16)
        attl, cm_atl = sb(f"attl{tb}", [128, NKC * TBW], BF16)
        qsb, cm_qsb = sb(f"qsb{tb}", [128, NKC * TBW], F32)
        wqf, cm_wqf = sb(f"wqf{tb}", [128, 2048], F32)
        wqh = Pair(f"wqh{tb}", [128, 2048], BF16); cm_wqh = wqh
        wql = Pair(f"wql{tb}", [128, 2048], BF16); cm_wql = wql
        ps_q0, cm_pq0 = psum(f"ps_q0_{tb}", [128, TBW])
        ps_q1, cm_pq1 = psum(f"ps_q1_{tb}", [128, TBW])

        nwq0, nwqb0, nwqs0, nqmm0, nqe0 = (s_wq[0].n, s_wqb[0].n, s_wqs.n,
                                           s_qmm.n, s_qe.n)

        def entry_wait(e, tb=tb, prev_out_n=prev_out_n, prev_you=prev_you):
            if tb == 0:
                e.wait_ge(s_vke.sem, NGRP)
                e.wait_ge(s_vkmm.sem, NGRP)
            else:
                e.wait_ge(s_you[0].sem, prev_you[0])
                e.wait_ge(s_you[1].sem, prev_you[1])
                if prev_out_n:
                    e.wait_ge(s_out.sem, prev_out_n)

        @block.sync
        def _(e, nwqs0=nwqs0, nqmm0=nqmm0, entry_wait=entry_wait):
            entry_wait(e)
            for m in range(NKC):
                if m >= 1:
                    e.wait_ge(s_wqs.sem, nwqs0 + m)
                e.dma_start(out=wqf[:].rearrange(
                    "p (kc q) -> p kc q", kc=NKC),
                    in_=wqT[:, m * 128:(m + 1) * 128].rearrange(
                        "(kc p) q -> p kc q", p=128)
                    ).then_inc(s_wq[m % 2].sem, 16)
                if tb == 0 and m % 4 == 3:
                    tern.pump_sync(e, 4)
        s_wq[0].n += NKC * 8
        s_wq[1].n += NKC * 8

        @block.gpsimd
        def _(e, nwqs0=nwqs0, nqmm0=nqmm0, entry_wait=entry_wait):
            entry_wait(e)
            for m in range(NKC):
                if m >= 2:
                    e.wait_ge(s_qmm.sem, nqmm0 + (m - 1) * NKC)
                e.dma_start(out=wqh[m % 2][:].rearrange(
                    "p (kc q) -> p kc q", kc=NKC),
                    in_=wqT[:, m * 128:(m + 1) * 128].rearrange(
                        "(kc p) q -> p kc q", p=128)
                    ).then_inc(s_wqb[m % 2].sem, 16)
                if tb == 0 and m % 4 == 3:
                    tern.pump_gpsimd(e, 4)
        s_wqb[0].n += NKC * 8
        s_wqb[1].n += NKC * 8

        @block.vector
        def _(e, nwq0=nwq0, nwqb0=nwqb0, nqmm0=nqmm0, entry_wait=entry_wait):
            em = Emit(e)
            entry_wait(e)
            for m in range(NKC):
                e.wait_ge(s_wq[m % 2].sem, nwq0 + 16 * (m // 2 + 1))
                e.wait_ge(s_wqb[m % 2].sem, nwqb0 + 16 * (m // 2 + 1))
                if m >= 2:
                    e.wait_ge(s_qmm.sem, nqmm0 + (m - 1) * NKC)
                em.tt(wql[m % 2], wqf[:], wqh[m % 2],
                      ALU.subtract).then_inc(s_wqs.sem, 1)
                if tb == 0 and m % 4 == 3:
                    tern.pump_vector(e, em, 4)
        s_wqs.n += NKC

        @block.tensor
        def _(e, t0=t0, nwqs0=nwqs0, nqe0=nqe0, entry_wait=entry_wait):
            entry_wait(e)
            for i in range(NKC * NKC):
                m, kc = divmod(i, NKC)
                psq = [ps_q0, ps_q1][m % 2]
                if kc == 0 and m >= 2:
                    e.wait_ge(s_qe.sem, nqe0 + m - 1)
                if kc == 0:
                    e.wait_ge(s_wqs.sem, nwqs0 + m + 1)
                b = m % 2
                xh = xnh[:, kc * OWN + t0: kc * OWN + t0 + TBW]
                xl = xnl[:, kc * OWN + t0: kc * OWN + t0 + TBW]
                wh_ = wqh[b][:, kc * 128:(kc + 1) * 128]
                wl_ = wql[b][:, kc * 128:(kc + 1) * 128]
                e.matmul(psq[:], wh_, xh, start=(kc == 0), stop=False)
                e.matmul(psq[:], wl_, xh, start=False, stop=False)
                e.matmul(psq[:], wh_, xl, start=False,
                         stop=(kc == NKC - 1)).then_inc(s_qmm.sem, 1)
                if tb == 0 and kc == NKC - 1 and m % 4 == 3:
                    tern.pump_tensor(e, 4)
        s_qmm.n += NKC * NKC

        @block.scalar
        def _(e, nqmm0=nqmm0):
            em = Emit(e)
            for m in range(NKC):
                e.wait_ge(s_qmm.sem, nqmm0 + NKC * (m + 1))
                psq = [ps_q0, ps_q1][m % 2]
                em.act(qsb[:, m * TBW:(m + 1) * TBW], psq[:],
                       ACT.Relu).then_inc(s_qe.sem, 1)
                if tb == 0 and m % 4 == 3:
                    tern.pump_scalar(e, em, 4)
        s_qe.n += NKC

        if debug and tb == 0:
            @block.gpsimd
            def _(e, nqe0=nqe0):
                e.wait_ge(s_qe.sem, nqe0 + NKC)
                e.dma_start(out=dbg["d_q"][:, :], in_=qsb[:]).then_inc(s_out.sem, 16)
            s_out.n += 16

        nrm, cm_nrm = sb(f"nrm{tb}", [4, TBW], F32)
        nrh, cm_nrh = sb(f"nrh{tb}", [4, TBW], BF16)
        nrl, cm_nrl = sb(f"nrl{tb}", [4, TBW], BF16)
        rbs, cm_rbs = sb(f"rbs{tb}", [128, TBW], F32)
        attf, cm_attf = sb(f"attf{tb}", [128, TBW], F32)
        ps_mn, cm_pmn = psum(f"ps_mn{tb}", [128, TBW])
        ps_nm, cm_pnm = psum(f"ps_nm{tb}", [4, TBW])
        ps_rc, cm_prc = psum(f"ps_rc{tb}", [128, TBW])

        noe0, nov0 = s_oemm.n, s_oev.n

        @block.tensor
        def _(e, noe0=noe0, nov0=nov0, nqe0=nqe0):
            for g in range(NGRP):
                e.wait_ge(s_qe.sem, nqe0 + g + 1)
                if g >= 1:
                    e.wait_ge(s_oev.sem, nov0 + 2 * g)
                q = qsb[:, g * TBW:(g + 1) * TBW]
                e.matmul(ps_nm[0:4, :], ncol[:, g * 4:(g + 1) * 4], q,
                         start=True, stop=True).then_inc(s_oemm.sem, 1)
                e.matmul(ps_mn[:], bd[:, g * 128:(g + 1) * 128], q,
                         start=True, stop=True)
                e.wait_ge(s_oev.sem, nov0 + 2 * g + 1)
                e.matmul(ps_rc[:], ind4[:], nrh[:], start=True, stop=False)
                e.matmul(ps_rc[:], ind4[:], nrl[:], start=False,
                         stop=True).then_inc(s_oemm.sem, 1)
                if tb == 0 and g % 4 == 3:
                    tern.pump_tensor(e, {3: 5, 7: 5, 11: 5, 15: 4}[g])
        s_oemm.n += NGRP * 2

        @block.vector
        def _(e, noe0=noe0):
            em = Emit(e)
            for g in range(NGRP):
                e.wait_ge(s_oemm.sem, noe0 + 2 * g + 1)
                em.ts(nrm[:], ps_nm[:], ATT_EPS, ALU.add)
                em.recip(nrm[:], nrm[:])
                em.copy(nrh[:], nrm[:])
                em.tt(nrl[:], nrm[:], nrh[:], ALU.subtract).then_inc(s_oev.sem, 1)
                e.wait_ge(s_oemm.sem, noe0 + 2 * g + 2)
                em.copy(rbs[:], ps_rc[:])
                em.tt(attf[:], ps_mn[:], rbs[:], ALU.mult)
                em.copy(atth[:, g * TBW:(g + 1) * TBW], attf[:])
                em.tt(attl[:, g * TBW:(g + 1) * TBW], attf[:],
                      atth[:, g * TBW:(g + 1) * TBW],
                      ALU.subtract).then_inc(s_oev.sem, 1)
                if tb == 0 and g % 4 == 3:
                    tern.pump_vector(e, em, {3: 5, 7: 5, 11: 4, 15: 4}[g])
        s_oev.n += NGRP * 2

        if tb == 0:
            @block.scalar
            def _(e):
                em = Emit(e)
                tern.pump_scalar(e, em, 19)

            @block.sync
            def _(e):
                tern.pump_sync(e, 20)

            @block.gpsimd
            def _(e):
                tern.pump_gpsimd(e, 17)
            assert tern.ig == NT_T and tern.iv == NT_T \
                and tern.ipe == NT_T and tern.isc == NT_T \
                and tern.iwb == NT_T, \
                (tern.ig, tern.iv, tern.ipe, tern.isc, tern.iwb)

        if debug and tb == 0:
            @block.gpsimd
            def _(e, nov0=nov0):
                e.wait_ge(s_oev.sem, nov0 + 2 * NGRP)
                e.dma_start(out=dbg["d_atth"][:, :],
                            in_=atth[:]).then_inc(s_out.sem, 16)
            s_out.n += 16

        free(cm_pmn, cm_pnm, cm_prc)
        free(cm_pq0, cm_pq1)
        free(cm_qsb, cm_wqf, cm_wqh, cm_wql, cm_nrm, cm_nrh, cm_nrl,
             cm_rbs, cm_attf)

        nout_dbg = s_out.n
        ffnr, cm_ffnr = sb(f"ffnr{tb}", [128, NKC * TBW], F32)
        gh, cm_gh = sb(f"gh{tb}", [128, (HS // 128) * TBW], BF16)
        gl, cm_gl = sb(f"gl{tb}", [128, (HS // 128) * TBW], BF16)
        w1r = Pair(f"w1r{tb}", [128, 16 * 128], BF16); cm_w1r = w1r
        w2r = Pair(f"w2r{tb}", [128, 2048], BF16); cm_w2r = w2r
        grf, cm_grf = sb(f"grf{tb}", [128, TBW], F32)
        ggf = Pair(f"ggf{tb}", [128, TBW], F32); cm_ggf = ggf
        gg2 = Ring(f"gg2{tb}", [128, TBW], F32, 2); cm_gg2 = gg2
        msv, cm_msv = sb(f"msv{tb}", [1, TBW], F32)
        ps_rs, cm_prs = psum(f"ps_rs{tb}", [128, TBW])
        ps_fc = []
        cm_fcs = []
        for z in range(2):
            t, cm = psum(f"ps_fc{z}_{tb}", [128, TBW])
            ps_fc.append(t)
            cm_fcs.append(cm)
        ps_fp0, cm_pfp0 = psum(f"ps_fp0_{tb}", [128, TBW])
        ps_fp1, cm_pfp1 = psum(f"ps_fp1_{tb}", [128, TBW])

        nw1r0 = s_w1r[0].n
        nfc0 = s_fcmm.n
        nfce0 = s_fce.n
        ngle0 = s_gle.n
        nw2r0 = s_w2r[0].n
        nfp0 = s_fpmm.n
        nfpe0 = s_fpe.n
        nms0 = s_msv.n
        nrsb0 = s_rsb.n

        @block.vector
        def _(e, nout_dbg=nout_dbg):
            em = Emit(e)
            if debug:
                e.wait_ge(s_out.sem, nout_dbg)
            em.memset(ffnr[:], 0.0)
            e.drain()

        @block.sync
        def _(e, nw1r0=nw1r0, nfc0=nfc0, nov0=nov0, nout_dbg=nout_dbg):
            for s4 in s_tw:
                e.wait_ge(s4.sem, (NT_T // 2) * 16)
            e.wait_ge(s_oev.sem, nov0 + 2 * NGRP)
            if debug:
                e.wait_ge(s_out.sem, nout_dbg)
            for m in range(NHC):
                if m >= 2:
                    e.wait_ge(s_fcmm.sem, nfc0 + m - 1)
                e.dma_start(out=w1r[m % 2],
                            in_=w1s[m, :, :, :].rearrange("t p q -> p t q")
                            ).then_inc(s_w1r[m % 2].sem, 16)
        s_w1r[0].n += (NHC // 2) * 16
        s_w1r[1].n += (NHC // 2) * 16

        @block.tensor
        def _(e, t0=t0, nw1r0=nw1r0, nfce0=nfce0, ngle0=ngle0, nw2r0=nw2r0,
              nfpe0=nfpe0, nrsb0=nrsb0, nms0=nms0):
            ip = 0
            for hs in range(NHS):
                for mloc in range(8):
                    m = hs * 8 + mloc
                    e.wait_ge(s_w1r[m % 2].sem, nw1r0 + 16 * (m // 2 + 1))
                    if m >= 2:
                        e.wait_ge(s_fce.sem, nfce0 + m - 1)
                    for kc in range(NKC):
                        xhk = xnh[:, kc * OWN + t0: kc * OWN + t0 + TBW]
                        xlk = xnl[:, kc * OWN + t0: kc * OWN + t0 + TBW]
                        w = w1r[m % 2][:, kc * 128:(kc + 1) * 128]
                        e.matmul(ps_fc[m % 2][:], w, xhk, start=(kc == 0), stop=False)
                        ins = e.matmul(ps_fc[m % 2][:], w, xlk, start=False,
                                       stop=(kc == NKC - 1))
                    ins.then_inc(s_fcmm.sem, 1)
                    if m >= 1:
                        e.wait_ge(s_fce.sem, nfce0 + m)
                        e.matmul(ps_rs[0:1, :], onesf[:], gg2[(m - 1) % 2],
                                 start=(m == 1),
                                 stop=False).then_inc(s_msv.sem, 1)
                e.wait_ge(s_gle.sem, ngle0 + 8 * (hs + 1))
                for mcp in range(8):
                    if hs or mcp:
                        e.wait_ge(s_fpe.sem, nfpe0 + 2 * (hs * 8 + mcp - 1) + 2)
                    iblk = hs * 8 + mcp
                    e.wait_ge(s_w2r[iblk % 2].sem, nw2r0 + 32 * (iblk // 2 + 1))
                    b2 = iblk % 2
                    for khl in range(8):
                        for half2 in range(2):
                            w = w2r[b2][:, khl * 256 + half2 * 128:
                                        khl * 256 + (half2 + 1) * 128]
                            gsl = slice(khl * TBW, (khl + 1) * TBW)
                            psf = [ps_fp0, ps_fp1][half2]
                            e.matmul(psf[:], w, gh[:, gsl],
                                     start=(khl == 0), stop=False)
                            ins = e.matmul(psf[:], w, gl[:, gsl], start=False,
                                           stop=(khl == 7))
                        ins.then_inc(s_fpmm.sem, 1)
                    ip += 1
            e.wait_ge(s_fce.sem, nfce0 + NHC)
            e.matmul(ps_rs[0:1, :], onesf[:], gg2[(NHC - 1) % 2],
                     start=False, stop=True).then_inc(s_msv.sem, 1)
            e.wait_ge(s_rsb.sem, nrsb0 + 3)
            e.matmul(ps_rs[:], ones1[:], rstd_b[0:1, t0:t0 + TBW],
                     start=True, stop=False)
            e.matmul(ps_rs[:], ones1[:], rstd_b[0:1, OWN + t0:OWN + t0 + TBW],
                     start=False, stop=True).then_inc(s_rsb.sem, 1)
        s_fcmm.n += NHC
        s_msv.n += NHC
        s_fpmm.n += NHS * 8 * 8
        s_rsb.n += 1

        @block.scalar
        def _(e, nfc0=nfc0, ngle0=ngle0, nms0=nms0, nfp0=nfp0, nrsb0=nrsb0,
              nov0=nov0, nout_dbg=nout_dbg):
            em = Emit(e)
            e.wait_ge(s_oev.sem, nov0 + 2 * NGRP)
            if debug:
                e.wait_ge(s_out.sem, nout_dbg)
            ip = 0
            for hs in range(NHS):
                for mloc in range(8):
                    m = hs * 8 + mloc
                    e.wait_ge(s_fcmm.sem, nfc0 + m + 1)
                    if m >= 2:
                        e.wait_ge(s_gle.sem, ngle0 + m - 1)
                        e.wait_ge(s_msv.sem, nms0 + m - 1)
                    em.act(grf[:], ps_fc[m % 2][:], ACT.Relu)
                    em.act(ggf[m % 2], grf[:], ACT.Square)
                    em.act(gh[:, mloc * TBW:(mloc + 1) * TBW], grf[:], ACT.Square)
                    em.act(gg2[m % 2], ggf[m % 2],
                           ACT.Square).then_inc(s_fce.sem, 1)
                for mcp in range(8):
                    if ip >= 2:
                        e.wait_ge(s_fpmm.sem, nfp0 + (ip - 1) * 8)
                    for t_ in range(2):
                        e.dma_start(
                            out=w2r[ip % 2][:].rearrange(
                                "p (k t q) -> p k t q", k=8, t=2)[:, :, t_, :],
                            in_=w2s[hs * 8:(hs + 1) * 8, mcp * 2 + t_, :, :]
                            .rearrange("k p q -> p k q")
                            ).then_inc(s_w2r[ip % 2].sem, 16)
                    ip += 1
            e.wait_ge(s_rsb.sem, nrsb0 + 1)
            em.act(msv[:], msv[:], ACT.Sqrt, bias=epst[0:1, 1:2]).then_inc(s_rsb.sem, 1)
        s_fce.n += NHC
        s_w2r[0].n += NHS * 4 * 32
        s_w2r[1].n += NHS * 4 * 32
        s_rsb.n += 1

        @block.vector
        def _(e, t0=t0, nfce0=nfce0, nfp0=nfp0, nfpe0=nfpe0, nms0=nms0,
              nrsb0=nrsb0):
            em = Emit(e)
            for hs in range(NHS):
                for mloc in range(8):
                    m = hs * 8 + mloc
                    e.wait_ge(s_fce.sem, nfce0 + m + 1)
                    em.tt(gl[:, mloc * TBW:(mloc + 1) * TBW], ggf[m % 2],
                          gh[:, mloc * TBW:(mloc + 1) * TBW],
                          ALU.subtract).then_inc(s_gle.sem, 1)
                for mcp in range(8):
                    e.wait_ge(s_fpmm.sem, nfp0 + (hs * 8 + mcp) * 8 + 8)
                    for half2 in range(2):
                        mc = mcp * 2 + half2
                        psf = [ps_fp0, ps_fp1][half2]
                        em.tt(ffnr[:, mc * TBW:(mc + 1) * TBW],
                              ffnr[:, mc * TBW:(mc + 1) * TBW], psf[:],
                              ALU.add).then_inc(s_fpe.sem, 1)
            e.wait_ge(s_msv.sem, nms0 + NHC)
            em.ts(msv[:], ps_rs[0:1, :], INV_HID, ALU.mult).then_inc(s_rsb.sem, 1)
            e.wait_ge(s_rsb.sem, nrsb0 + 2)
            em.recip(msv[:], msv[:])
            em.copy(rstd_b[0:1, t0:t0 + TBW], msv[:])
            em.tt(rstd_b[0:1, OWN + t0:OWN + t0 + TBW], msv[:],
                  rstd_b[0:1, t0:t0 + TBW], ALU.subtract).then_inc(s_rsb.sem, 1)
        s_gle.n += NHC
        s_fpe.n += NHS * 16
        s_rsb.n += 2

        if debug and tb == 0:
            @block.gpsimd
            def _(e, nfpe0=nfpe0):
                e.wait_ge(s_fpe.sem, nfpe0 + NHS * 16)
                e.dma_start(out=dbg["d_ffnr"][:, :],
                            in_=ffnr[:]).then_inc(s_out.sem, 16)
            s_out.n += 16

        free(*cm_fcs, cm_pfp0, cm_pfp1)
        free(cm_gh, cm_gl, cm_w1r, cm_w2r, cm_grf, cm_ggf, cm_gg2, cm_msv)

        wpf, cm_wpf = sb(f"wpf{tb}", [128, 2048], F32)
        wph = Pair(f"wph{tb}", [128, 2048], BF16); cm_wph = wph
        wpl = Pair(f"wpl{tb}", [128, 2048], BF16); cm_wpl = wpl
        xre = Pair(f"xre{tb}", [128, TBW], F32); cm_xre = xre
        fin1, cm_f1 = sb(f"fin1{tb}", [128, TBW], F32)
        outb = Pair(f"outb{tb}", [128, TBW], F32); cm_ob = outb
        ps_p0, cm_pp0b = psum(f"ps_p0_{tb}", [128, TBW])
        ps_p1, cm_pp1b = psum(f"ps_p1_{tb}", [128, TBW])

        nwp0, nwpb0, nwps0, npmm0, nfin0 = (s_wp[0].n, s_wpb[0].n, s_wps.n,
                                            s_pmm.n, s_fin.n)
        nxre0 = s_xre[0].n
        nout0 = s_out.n
        nyou0 = (s_you[0].n, s_you[1].n)

        @block.sync
        def _(e, t0=t0, nwps0=nwps0, npmm0=npmm0, nfin0=nfin0, nfpe0=nfpe0):
            e.wait_ge(s_fpe.sem, nfpe0 + NHS * 16)

            def xre_load(mx):
                if mx >= 2:
                    e.wait_ge(s_fin.sem, nfin0 + mx - 1)
                e.dma_start(out=xre[mx % 2],
                            in_=xT[mx * 128:(mx + 1) * 128, t0:t0 + TBW]
                            ).then_inc(s_xre[mx % 2].sem, 16)

            for m in range(NKC):
                if m >= 2:
                    xre_load(m - 2)
                if m >= 1:
                    e.wait_ge(s_wps.sem, nwps0 + m)
                e.dma_start(out=wpf[:].rearrange(
                    "p (kc q) -> p kc q", kc=NKC),
                    in_=wpT[:, m * 128:(m + 1) * 128].rearrange(
                        "(kc p) q -> p kc q", p=128)
                    ).then_inc(s_wp[m % 2].sem, 16)
            xre_load(NKC - 2)
            xre_load(NKC - 1)
        s_wp[0].n += NKC * 8
        s_wp[1].n += NKC * 8
        s_xre[0].n += NKC * 8
        s_xre[1].n += NKC * 8

        @block.gpsimd
        def _(e, t0=t0, nwps0=nwps0, npmm0=npmm0, nfpe0=nfpe0, nfin0=nfin0):
            e.wait_ge(s_fpe.sem, nfpe0 + NHS * 16)

            def y_dma(my):
                e.wait_ge(s_fin.sem, nfin0 + my + 1)
                e.dma_start(out=y[my * 128:(my + 1) * 128, t0:t0 + TBW],
                            in_=outb[my % 2][:]).then_inc(s_you[my % 2].sem, 16)

            for m in range(NKC):
                if m >= 2:
                    y_dma(m - 2)
                    e.wait_ge(s_pmm.sem, npmm0 + (m - 1) * NKC)
                e.dma_start(out=wph[m % 2][:].rearrange(
                    "p (kc q) -> p kc q", kc=NKC),
                    in_=wpT[:, m * 128:(m + 1) * 128].rearrange(
                        "(kc p) q -> p kc q", p=128)
                    ).then_inc(s_wpb[m % 2].sem, 16)
            y_dma(NKC - 2)
            y_dma(NKC - 1)
        s_wpb[0].n += NKC * 8
        s_wpb[1].n += NKC * 8
        s_you[0].n += NKC * 8
        s_you[1].n += NKC * 8

        @block.vector
        def _(e, t0=t0, nwp0=nwp0, nwpb0=nwpb0, npmm0=npmm0, nfin0=nfin0,
              nout0=nout0, nxre0=nxre0, nyou0=nyou0):
            em = Emit(e)

            def final(m):
                e.wait_ge(s_pmm.sem, npmm0 + NKC * (m + 1))
                e.wait_ge(s_xre[m % 2].sem, nxre0 + 16 * (m // 2 + 1))
                if m >= 2:
                    e.wait_ge(s_you[m % 2].sem, nyou0[m % 2] + 16 * (m // 2))
                psp = [ps_p0, ps_p1][m % 2]
                em.tt(fin1[:], ffnr[:, m * TBW:(m + 1) * TBW], ps_rs[:], ALU.mult)
                em.tt(fin1[:], fin1[:], psp[:], ALU.add)
                em.stt(outb[m % 2], fin1[:], bpt[:, m:m + 1], xre[m % 2],
                       ALU.add, ALU.add).then_inc(s_fin.sem, 1)

            for m in range(NKC):
                if m >= 2:
                    final(m - 2)
                e.wait_ge(s_wp[m % 2].sem, nwp0 + 16 * (m // 2 + 1))
                e.wait_ge(s_wpb[m % 2].sem, nwpb0 + 16 * (m // 2 + 1))
                if m >= 2:
                    e.wait_ge(s_pmm.sem, npmm0 + (m - 1) * NKC)
                em.tt(wpl[m % 2], wpf[:], wph[m % 2],
                      ALU.subtract).then_inc(s_wps.sem, 1)
            for m in range(NKC - 2, NKC):
                final(m)
        s_wps.n += NKC
        s_fin.n += NKC

        @block.tensor
        def _(e, nwps0=nwps0, nfin0=nfin0, nrsb0=nrsb0):
            e.wait_ge(s_rsb.sem, nrsb0 + 4)
            for i in range(NKC * NKC):
                m, kc = divmod(i, NKC)
                psp = [ps_p0, ps_p1][m % 2]
                if kc == 0 and m >= 2:
                    e.wait_ge(s_fin.sem, nfin0 + m - 1)
                if kc == 0:
                    e.wait_ge(s_wps.sem, nwps0 + m + 1)
                b = m % 2
                ah = atth[:, kc * TBW:(kc + 1) * TBW]
                al = attl[:, kc * TBW:(kc + 1) * TBW]
                wh_ = wph[b][:, kc * 128:(kc + 1) * 128]
                wl_ = wpl[b][:, kc * 128:(kc + 1) * 128]
                e.matmul(psp[:], wh_, ah, start=(kc == 0), stop=False)
                e.matmul(psp[:], wl_, ah, start=False, stop=False)
                e.matmul(psp[:], wh_, al, start=False,
                         stop=(kc == NKC - 1)).then_inc(s_pmm.sem, 1)
        s_pmm.n += NKC * NKC



        free(cm_pp0b, cm_pp1b)
        free(cm_prs)
        free(cm_ath, cm_atl, cm_ffnr, cm_wpf, cm_wph, cm_wpl, cm_xre,
             cm_f1, cm_ob)

    @block.gpsimd
    def _(e):
        e.wait_ge(s_you[0].sem, s_you[0].n)
        e.wait_ge(s_you[1].sem, s_you[1].n)
        if s_out.n:
            e.wait_ge(s_out.sem, s_out.n)

    block_cm.__exit__(None, None, None)
    return nc, dbg


_CONSTS = None


def _consts():
    global _CONSTS
    if _CONSTS is None:
        sel = np.zeros((128, 2), ml_dtypes.bfloat16)
        for p in range(128):
            sel[p, p // 64] = ml_dtypes.bfloat16(1.0 / 64.0)
        ind2 = np.zeros((2, 128), ml_dtypes.bfloat16)
        for p in range(128):
            ind2[p // 64, p] = ml_dtypes.bfloat16(1.0)
        ind4 = np.zeros((4, 128), ml_dtypes.bfloat16)
        for p in range(128):
            ind4[p // 32, p] = ml_dtypes.bfloat16(1.0)
        ones1 = np.ones((1, 128), ml_dtypes.bfloat16)
        onesf = np.ones((128, 1), np.float32)
        idn = np.eye(128, dtype=np.float32)
        idnb = np.eye(128, dtype=np.float32).astype(ml_dtypes.bfloat16)
        eps = np.array([[LN_EPS, RMS_EPS]], np.float32)
        _CONSTS = dict(seld=sel, ind2d=ind2, ind4d=ind4, ones1d=ones1,
                       onesfd=onesf, idnd=idn, idnbd=idnb, epsd=eps)
    return _CONSTS


def make_in_maps(x, wq, wp, wf, w2, bp):
    consts = _consts()
    in_maps = []
    for core in range(8):
        b, half = divmod(core, 2)
        xt = np.asarray(x[b].T)
        if half == 0:
            xcore = np.ascontiguousarray(xt)
        else:
            xcore = np.ascontiguousarray(
                np.concatenate([xt[:, OWN:], xt[:, :OWN]], axis=1))
        im = dict(xT=xcore, wqT=wq, wpT=wp, wfn=wf, w2n=w2, bpd=bp)
        im.update(consts)
        in_maps.append(im)
    return in_maps


def kernel(L0_attn_x, L0_attn_w_qkv, L0_attn_w_proj, L0_attn_b_proj,
           L0_ffn_W_fc, L0_ffn_W_proj):
    x = np.asarray(L0_attn_x, dtype=np.float32)
    wq = np.ascontiguousarray(np.asarray(L0_attn_w_qkv, np.float32).T)
    wp = np.ascontiguousarray(np.asarray(L0_attn_w_proj, np.float32).T)
    wf = np.ascontiguousarray(np.asarray(L0_ffn_W_fc, np.float32))
    w2 = np.ascontiguousarray(np.asarray(L0_ffn_W_proj, np.float32))
    bp = np.ascontiguousarray(
        np.asarray(L0_attn_b_proj, np.float32).reshape(NKC, 128).T)

    nc, _ = build_program(debug=False)
    in_maps = make_in_maps(x, wq, wp, wf, w2, bp)
    res = run_bass_kernel_spmd(nc, in_maps, core_ids=list(range(8)))
    out = np.empty((4, TOK, C), np.float32)
    for core in range(8):
        b, half = divmod(core, 2)
        yc = res.results[core]["y"]
        out[b, half * OWN:(half + 1) * OWN, :] = yc.T
    return out


# version 13
# speedup vs baseline: 1.0119x; 1.0119x over previous
import numpy as np
import ml_dtypes
import concourse.bass as bass
import concourse.mybir as mybir
from concourse.bass_utils import run_bass_kernel_spmd

F32 = mybir.dt.float32
BF16 = mybir.dt.bfloat16
ALU = mybir.AluOpType
ACT = mybir.ActivationFunctionType

C = 2048
NKC = 16
TOK = 2048
OWN = 1024
TBW = 512
NTB = OWN // TBW
D = 32
E = 33
NGRP = 16
HID = 8192
NHC = HID // 128
HS = 1024
NHS = HID // HS
THS = 512
LN_EPS = 1e-5
ATT_EPS = 1e-15
RMS_EPS = float(np.finfo(np.float32).eps)
INV_C = 1.0 / C
INV_HID = 1.0 / HID


class Emit:

    def __init__(self, eng):
        self.e = eng
        self.dirty_w = set()
        self.dirty_r = set()

    def _names(self, aps):
        out = set()
        for a in aps:
            if a is not None and hasattr(a, "tensor"):
                out.add(a.tensor.name)
        return out

    def _pre(self, ins, outs):
        rd = self._names(ins)
        wr = self._names(outs)
        if (rd & self.dirty_w) or (wr & (self.dirty_w | self.dirty_r)):
            self.e.drain()
            self.dirty_w = set()
            self.dirty_r = set()
        self.dirty_r |= rd

    def _post(self, outs):
        self.dirty_w |= self._names(outs)

    def copy(self, out, in_):
        self._pre([in_], [out])
        if hasattr(self.e, "tensor_copy"):
            r = self.e.tensor_copy(out, in_)
        else:
            r = self.e.copy(out, in_)
        self._post([out])
        return r

    def ts(self, out, in_, s1, op0, s2=None, op1=None):
        self._pre([in_], [out])
        if op1 is None:
            r = self.e.tensor_scalar(out, in_, s1, None, op0)
        else:
            r = self.e.tensor_scalar(out, in_, s1, s2, op0, op1)
        self._post([out])
        return r

    def tt(self, out, a, b, op):
        self._pre([a, b], [out])
        r = self.e.tensor_tensor(out, a, b, op)
        self._post([out])
        return r

    def stt(self, out, in0, scalar, in1, op0, op1):
        self._pre([in0, in1, scalar if hasattr(scalar, "tensor") else None], [out])
        r = self.e.scalar_tensor_tensor(out, in0, scalar, in1, op0, op1)
        self._post([out])
        return r

    def act(self, out, in_, func, bias=None, scale=None):
        self._pre([in_], [out])
        kw = {}
        if bias is not None:
            kw["bias"] = bias
        if scale is not None:
            kw["scale"] = scale
        r = self.e.activation(out, in_, func, **kw)
        self._post([out])
        return r

    def recip(self, out, in_):
        self._pre([in_], [out])
        r = self.e.reciprocal(out, in_)
        self._post([out])
        return r

    def memset(self, out, val):
        self._pre([], [out])
        r = self.e.memset(out, val)
        self._post([out])
        return r


class Sem:
    def __init__(self, nc, name):
        self.cm = nc.semaphore(name)
        self.sem = self.cm.__enter__()
        self.n = 0


def build_program(debug=False):
    nc = bass.Bass("TRN2", target_bir_lowering=False)

    def sb(name, shape, dt):
        cm = nc.sbuf_tensor(name, shape, dt)
        return cm.__enter__(), cm

    def psum(name, shape, dt=F32):
        cm = nc.psum_tensor(name, shape, dt)
        return cm.__enter__(), cm

    class Ring:
        def __init__(self, name, shape, dt, k):
            self.cms = [nc.sbuf_tensor(f"{name}_{i}", shape, dt) for i in range(k)]
            self.ts = [cm.__enter__() for cm in self.cms]

        def __getitem__(self, i):
            return self.ts[i][:]

        def __exit__(self, *a):
            for cm in reversed(self.cms):
                cm.__exit__(None, None, None)

    class Pair:
        def __init__(self, name, shape, dt):
            self.cms = [nc.sbuf_tensor(f"{name}_{i}", shape, dt) for i in range(2)]
            self.ts = [cm.__enter__() for cm in self.cms]

        def __getitem__(self, i):
            return self.ts[i][:]

        def __exit__(self, *a):
            self.cms[1].__exit__(None, None, None)
            self.cms[0].__exit__(None, None, None)

    def free(*cms):
        for cm in reversed(cms):
            cm.__exit__(None, None, None)

    xT = nc.dram_tensor("xT", [C, TOK], F32, kind="ExternalInput").ap()
    wqT = nc.dram_tensor("wqT", [C, 3 * C], F32, kind="ExternalInput").ap()
    wpT = nc.dram_tensor("wpT", [C, C], F32, kind="ExternalInput").ap()
    wfn = nc.dram_tensor("wfn", [HID, C], F32, kind="ExternalInput").ap()
    w2n = nc.dram_tensor("w2n", [C, HID], F32, kind="ExternalInput").ap()
    bpd = nc.dram_tensor("bpd", [128, NKC], F32, kind="ExternalInput").ap()
    seld = nc.dram_tensor("seld", [128, 2], BF16, kind="ExternalInput").ap()
    ind2d = nc.dram_tensor("ind2d", [2, 128], BF16, kind="ExternalInput").ap()
    ind4d = nc.dram_tensor("ind4d", [4, 128], BF16, kind="ExternalInput").ap()
    ones1d = nc.dram_tensor("ones1d", [1, 128], BF16, kind="ExternalInput").ap()
    onesfd = nc.dram_tensor("onesfd", [128, 1], F32, kind="ExternalInput").ap()
    idnd = nc.dram_tensor("idnd", [128, 128], F32, kind="ExternalInput").ap()
    idnbd = nc.dram_tensor("idnbd", [128, 128], BF16, kind="ExternalInput").ap()
    epsd = nc.dram_tensor("epsd", [1, 2], F32, kind="ExternalInput").ap()
    y = nc.dram_tensor("y", [C, OWN], F32, kind="ExternalOutput").ap()
    w1s = nc.dram_tensor("w1s", [NHC, NKC, 128, 128], BF16, kind="Internal").ap()
    w2s = nc.dram_tensor("w2s", [NHC, NKC, 128, 128], BF16, kind="Internal").ap()
    dbg = {}
    if debug:
        for nm, sh, dt in [
            ("d_xnh", [128, NKC * OWN], BF16), ("d_xnl", [128, NKC * OWN], BF16),
            ("d_bd", [128, NGRP * 128], F32), ("d_ncol", [128, NGRP * 4], F32),
            ("d_q", [128, NKC * TBW], F32), ("d_atth", [128, NKC * TBW], BF16),
            ("d_ffnr", [128, NKC * TBW], F32),
        ]:
            dbg[nm] = nc.dram_tensor(nm, sh, dt, kind="ExternalOutput").ap()

    sel, _ = sb("sel", [128, 2], BF16)
    ind2, _ = sb("ind2", [2, 128], BF16)
    ind4, _ = sb("ind4", [4, 128], BF16)
    ones1, _ = sb("ones1", [1, 128], BF16)
    onesf, _ = sb("onesf", [128, 1], F32)
    idn, _ = sb("idn", [128, 128], F32)
    idnb, _ = sb("idnb", [128, 128], BF16)
    zt, _ = sb("zt", [128, 160], BF16)
    epst, _ = sb("epst", [1, 2], F32)
    bpt, _ = sb("bpt", [128, NKC], F32)
    xnh, _ = sb("xnh", [128, NKC * OWN], BF16)
    xnl, _ = sb("xnl", [128, NKC * OWN], BF16)
    bd, _ = sb("bd", [128, NGRP * 128], F32)
    ncol, _ = sb("ncol", [128, NGRP * 4], F32)
    rstd_b, _ = sb("rstd_b", [1, 2 * OWN], BF16)

    def S(name):
        return Sem(nc, name)

    s_cst = S("s_cst")
    s_wfb = [S(f"s_wfb{i}") for i in range(2)]
    s_tg = S("s_tg")
    s_tq = S("s_tq")
    s_tpm = S("s_tpm")
    s_tse = S("s_tse")
    s_tv = S("s_tv")
    s_tw = [S(f"s_tw{i}") for i in range(2)]
    s_ldx = [S("s_ldx0"), S("s_ldx1")]
    s_xnv = S("s_xnv")
    s_xnc = S("s_xnc")
    s_sq = S("s_sq")
    s_sqr = S("s_sqr")
    s_stat = S("s_stat")
    s_lnv = S("s_lnv")
    s_wkv = [S("s_wkv0"), S("s_wkv1")]
    s_wkvb = [S("s_wkvb0"), S("s_wkvb1")]
    s_wkvs = S("s_wkvs")
    s_kvmm = S("s_kvmm")
    s_kve = S("s_kve")
    s_tpmm = S("s_tpmm")
    s_tpe = S("s_tpe")
    s_vkmm = S("s_vkmm")
    s_vke = S("s_vke")
    s_wq = [S("s_wq0"), S("s_wq1")]
    s_wqb = [S("s_wqb0"), S("s_wqb1")]
    s_wqs = S("s_wqs")
    s_qmm = S("s_qmm")
    s_qe = S("s_qe")
    s_oemm = S("s_oemm")
    s_oev = S("s_oev")
    s_w1r = [S("s_w1r0"), S("s_w1r1")]
    s_fcmm = S("s_fcmm")
    s_fce = S("s_fce")
    s_gle = S("s_gle")
    s_w2r = [S("s_w2r0"), S("s_w2r1")]
    s_fpmm = S("s_fpmm")
    s_fpe = S("s_fpe")
    s_msv = S("s_msv")
    s_rsb = S("s_rsb")
    s_wp = [S("s_wp0"), S("s_wp1")]
    s_wpb = [S("s_wpb0"), S("s_wpb1")]
    s_wps = S("s_wps")
    s_pmm = S("s_pmm")
    s_fin = S("s_fin")
    s_xre = [S("s_xre0"), S("s_xre1")]
    s_out = S("s_out")
    s_you = [S("s_you0"), S("s_you1")]

    block_cm = nc.Block()
    block = block_cm.__enter__()

    @block.sync
    def _(e):
        for t, d in [(sel, seld), (ind2, ind2d), (ind4, ind4d), (ones1, ones1d),
                     (onesf, onesfd), (idn, idnd), (idnb, idnbd),
                     (bpt, bpd), (epst, epsd)]:
            e.dma_start(out=t[:], in_=d[:, :]).then_inc(s_cst.sem, 16)
    s_cst.n = 9 * 16

    @block.vector
    def _(e):
        em = Emit(e)
        e.wait_ge(s_cst.sem, s_cst.n)
        em.memset(zt[:], 0.0)
        em.memset(bd[:], 0.0)
        em.memset(ncol[:], 0.0)

    twb = Ring("twb", [128, 2048], BF16, 1); cm_twb = twb
    tred, cm_tred = sb("tred", [128, 32], F32)
    tscl, cm_tscl = sb("tscl", [128, 32], BF16)
    trec, cm_trec = sb("trec", [128, 32], F32)
    trat, cm_trat = sb("trat", [128, 2048], BF16)
    tga, cm_tga = sb("tga", [128, 2048], BF16)
    tw1n, cm_tw1n = sb("tw1n", [128, 2048], BF16)
    tw1t, cm_tw1t = sb("tw1t", [128, 16 * 128], BF16)
    ps_ta, cm_pta = psum("ps_ta", [128, 1024], BF16)
    ps_tb, cm_ptb = psum("ps_tb", [128, 1024], BF16)
    ps_tc, cm_ptc = psum("ps_tc", [128, 1024], BF16)

    tern_blocks = ([("w1", b) for b in range(HID // 128)]
                   + [("w2", rc) for rc in range(64)])
    NT_T = len(tern_blocks)

    def tern_src(blk):
        w, a = blk
        if w == "w1":
            return wfn[a * 128:(a + 1) * 128, :]
        r, c = divmod(a, 4)
        return w2n[r * 128:(r + 1) * 128, c * 2048:(c + 1) * 2048]

    def tern_dst(blk):
        w, a = blk
        if w == "w1":
            return w1s[a, :, :, :]
        r, c = divmod(a, 4)
        return w2s[16 * c:16 * (c + 1), r, :, :]

    class Tern:
        def __init__(self):
            self.ig = 0
            self.iv = 0
            self.ipe = 0
            self.isc = 0
            self.iwb = 0

        def pump_gpsimd(self, e, n):
            em = Emit(e)
            for _ in range(n):
                i = self.ig
                if i >= 1:
                    e.wait_ge(s_tv.sem, 2 * (i - 1) + 1)
                e.dma_start(out=twb[0],
                            in_=tern_src(tern_blocks[i])).then_inc(s_wfb[i % 2].sem, 16)
                self.ig += 1

        def pump_vector(self, e, em, n):
            for _ in range(n):
                i = self.iv
                b = i % 2
                e.wait_ge(s_wfb[b].sem, 16 * (i // 2 + 1))
                e.tensor_reduce(tred[:].rearrange("p (g o) -> p g o", o=1),
                                twb[0].rearrange("p (g o) -> p g o", o=64),
                                mybir.AxisListType.X, ALU.add,
                                apply_absolute_value=True)
                em.dirty_w.add("tred")
                em.ts(tscl[:], tred[:], 1.0 / 64.0, ALU.mult, 1e-8, ALU.max)
                em.recip(trec[:], tscl[:])
                rb = trec[:].rearrange("p (g o) -> p g o", o=1).broadcast_to([128, 32, 64])
                sb_ = tscl[:].rearrange("p (g o) -> p g o", o=1).broadcast_to([128, 32, 64])
                em.tt(trat[:].rearrange("p (g o) -> p g o", o=64),
                      twb[0].rearrange("p (g o) -> p g o", o=64), rb,
                      ALU.mult).then_inc(s_tv.sem, 1)
                em.ts(tga[:], trat[:], 0.5, ALU.is_gt)
                em.stt(tga[:], trat[:], -0.5, tga[:], ALU.is_ge, ALU.add)
                if i >= 1:
                    e.wait_ge(s_tpm.sem, 16 * i)
                em.stt(tw1n[:].rearrange("p (g o) -> p g o", o=64),
                       tga[:].rearrange("p (g o) -> p g o", o=64), -1.0, sb_,
                       ALU.add, ALU.mult).then_inc(s_tv.sem, 1)
                self.iv += 1

        def pump_tensor(self, e, n):
            for _ in range(n):
                i = self.ipe
                e.wait_ge(s_tv.sem, 2 * i + 2)
                for t in range(16):
                    it = 16 * i + t
                    pst = [ps_ta, ps_tb, ps_tc][it % 3]
                    if it >= 3:
                        e.wait_ge(s_tse.sem, it - 2)
                    e.transpose(pst[:, 0:128],
                                tw1n[:, t * 128:(t + 1) * 128],
                                idnb[:]).then_inc(s_tpm.sem, 1)
                self.ipe += 1

        def pump_scalar(self, e, em, n):
            for _ in range(n):
                i = self.isc
                for t in range(16):
                    it = 16 * i + t
                    e.wait_ge(s_tpm.sem, it + 1)
                    if t == 0 and i >= 1:
                        e.wait_ge(s_tw[(i - 1) % 2].sem, 16 * ((i - 1) // 2 + 1))
                    pst = [ps_ta, ps_tb, ps_tc][it % 3]
                    em.copy(tw1t[:, t * 128:(t + 1) * 128],
                            pst[:, 0:128]).then_inc(s_tse.sem, 1)
                self.isc += 1

        def pump_sync(self, e, n):
            for _ in range(n):
                i = self.iwb
                e.wait_ge(s_tse.sem, 16 * (i + 1))
                e.dma_start(out=tern_dst(tern_blocks[i]).rearrange("t p q -> p t q"),
                            in_=tw1t[:]).then_inc(s_tw[i % 2].sem, 16)
                self.iwb += 1

    tern = Tern()
    for j, s4 in enumerate(s_wfb):
        s4.n = (NT_T // 2) * 16
    s_tq.n = NT_T
    s_tv.n = NT_T * 2
    s_tpm.n = NT_T * 16
    s_tse.n = NT_T * 16
    s_tw[0].n = (NT_T // 2) * 16
    s_tw[1].n = (NT_T // 2) * 16

    @block.gpsimd
    def _(e):
        e.wait_ge(s_cst.sem, s_cst.n)
        tern.pump_gpsimd(e, 3)

    @block.vector
    def _(e):
        em = Emit(e)
        tern.pump_vector(e, em, 2)

    @block.tensor
    def _(e):
        e.wait_ge(s_cst.sem, s_cst.n)
        tern.pump_tensor(e, 1)

    @block.scalar
    def _(e):
        em = Emit(e)
        tern.pump_scalar(e, em, 1)

    xnph, cm_xnph = sb("xnph", [128, NKC * OWN], BF16)
    xnpl, cm_xnpl = sb("xnpl", [128, NKC * OWN], BF16)
    xf = Pair("xf", [128, 512], F32); cm_xf = xf
    sqb, cm_sqb = sb("sqb", [128, OWN], F32)
    lnhl, cm_lnhl = sb("lnhl", [1, 2048], BF16)
    statf, cm_statf = sb("statf", [1, 1536], F32)
    xnf = Pair("xnf", [128, 512], F32); cm_xnf = xnf

    def sqh(kc):
        return sqb[:, (kc % 2) * 512:(kc % 2) * 512 + 512]
    stat_m = statf[0:1, 0:512]
    stat_v = statf[0:1, 512:1024]
    stat_t = statf[0:1, 1024:1536]
    ps_sx, cm_psx = psum("ps_sx", [1, 512])
    ps_sxx, cm_psxx = psum("ps_sxx", [1, 512])
    ps_rb, cm_prb = psum("ps_rb", [128, 512])
    ps_mrb, cm_pmrb = psum("ps_mrb", [128, 512])

    NQ = 4

    def q_cols(qi):
        return slice(qi * 512, (qi + 1) * 512)

    @block.sync
    def _(e):
        def wait_consumers(j):
            qi, r = divmod(j, 2 * NKC)
            p, kc = divmod(r, NKC)
            if p == 0:
                e.wait_ge(s_sq.sem, qi * NKC + kc + 1)
                e.wait_ge(s_stat.sem, qi * (NKC + 1) + kc + 1)
            else:
                e.wait_ge(s_lnv.sem, qi * (2 + NKC) + 2 + kc + 1)

        j = 0
        for qi in range(NQ):
            for p in range(2):
                for kc in range(NKC):
                    if j >= 2:
                        wait_consumers(j - 2)
                    e.dma_start(out=xf[j % 2],
                                in_=xT[kc * 128:(kc + 1) * 128, q_cols(qi)]
                                ).then_inc(s_ldx[j % 2].sem, 16)
                    j += 1
            tern.pump_sync(e, 3)
    s_ldx[0].n = NQ * NKC * 16
    s_ldx[1].n = NQ * NKC * 16

    @block.scalar
    def _(e):
        em = Emit(e)
        for qi in range(NQ):
            h, qq = divmod(qi, 2)
            xdh = [xnh, xnph][h]
            c0 = qq * 512
            base = qi * 2 * NKC
            for kc in range(NKC):
                j_ = base + kc
                e.wait_ge(s_ldx[j_ % 2].sem, 16 * (j_ // 2 + 1))
                if kc >= 2:
                    e.wait_ge(s_stat.sem, qi * (NKC + 1) + kc - 1)
                elif qi >= 1:
                    e.wait_ge(s_stat.sem, (qi - 1) * (NKC + 1) + 15 + kc)
                em.act(sqh(kc), xf[j_ % 2],
                       ACT.Square).then_inc(s_sq.sem, 1)
            e.wait_ge(s_lnv.sem, qi * (2 + NKC) + 1)
            em.act(stat_t, stat_v, ACT.Sqrt,
                   bias=epst[0:1, 0:1]).then_inc(s_sqr.sem, 1)
            for kc in range(NKC):
                e.wait_ge(s_xnv.sem, qi * NKC + kc + 1)
                em.copy(xdh[:, kc * OWN + c0:kc * OWN + c0 + 512],
                        xnf[kc % 2]).then_inc(s_xnc.sem, 1)
            tern.pump_scalar(e, em, 3)
    s_sq.n = NQ * NKC
    s_sqr.n = NQ
    s_xnv.n = NQ * NKC
    s_xnc.n = NQ * NKC

    @block.tensor
    def _(e):
        for qi in range(NQ):
            for kc in range(NKC):
                e.wait_ge(s_sq.sem, qi * NKC + kc + 1)
                e.matmul(ps_sx[0:1, :], onesf[:], xf[(qi * 2 * NKC + kc) % 2],
                         start=(kc == 0), stop=(kc == NKC - 1))
                e.matmul(ps_sxx[0:1, :], onesf[:], sqh(kc),
                         start=(kc == 0),
                         stop=(kc == NKC - 1)).then_inc(s_stat.sem, 1)
            e.wait_ge(s_lnv.sem, qi * (2 + NKC) + 2)
            e.matmul(ps_rb[:], ones1[:], lnhl[0:1, 0:512],
                     start=True, stop=False)
            e.matmul(ps_rb[:], ones1[:], lnhl[0:1, 1024:1536],
                     start=False, stop=True)
            e.matmul(ps_mrb[:], ones1[:], lnhl[0:1, 512:1024],
                     start=True, stop=False)
            e.matmul(ps_mrb[:], ones1[:], lnhl[0:1, 1536:2048],
                     start=False, stop=True).then_inc(s_stat.sem, 1)
            tern.pump_tensor(e, 3)
    s_stat.n = NQ * (NKC + 1)



    @block.vector
    def _(e):
        em = Emit(e)
        for qi in range(NQ):
            h, qq = divmod(qi, 2)
            e.wait_ge(s_stat.sem, qi * (NKC + 1) + NKC)
            sm = stat_m
            sv = stat_v
            st = stat_t
            em.ts(sm, ps_sx[:], INV_C, ALU.mult)
            em.ts(sv, ps_sxx[:], INV_C, ALU.mult)
            em.tt(st, sm, sm, ALU.mult)
            em.tt(sv, sv, st, ALU.subtract).then_inc(s_lnv.sem, 1)
            e.wait_ge(s_sqr.sem, qi + 1)
            em.recip(st, st)
            em.tt(sm, sm, st, ALU.mult)
            em.copy(lnhl[0:1, 0:512], st)
            em.tt(lnhl[0:1, 1024:1536], st, lnhl[0:1, 0:512], ALU.subtract)
            em.copy(lnhl[0:1, 512:1024], sm)
            em.tt(lnhl[0:1, 1536:2048], sm, lnhl[0:1, 512:1024],
                  ALU.subtract).then_inc(s_lnv.sem, 1)
            e.wait_ge(s_stat.sem, qi * (NKC + 1) + NKC + 1)
            xdh = [xnh, xnph][h]
            xdl = [xnl, xnpl][h]
            c0 = qq * 512

            def fin_lo(kc):
                e.wait_ge(s_xnc.sem, qi * NKC + kc + 1)
                em.tt(xdl[:, kc * OWN + c0:kc * OWN + c0 + 512],
                      xnf[kc % 2],
                      xdh[:, kc * OWN + c0:kc * OWN + c0 + 512],
                      ALU.subtract).then_inc(s_lnv.sem, 1)

            for kc in range(NKC):
                j_ = (qi * 2 + 1) * NKC + kc
                e.wait_ge(s_ldx[j_ % 2].sem, 16 * (j_ // 2 + 1))
                em.tt(xnf[kc % 2], xf[j_ % 2], ps_rb[:], ALU.mult)
                em.tt(xnf[kc % 2], xnf[kc % 2], ps_mrb[:],
                      ALU.subtract).then_inc(s_xnv.sem, 1)
                if kc >= 1:
                    fin_lo(kc - 1)
            fin_lo(NKC - 1)
            tern.pump_vector(e, em, 3)
    s_lnv.n = NQ * (2 + NKC)

    @block.gpsimd
    def _(e):
        tern.pump_gpsimd(e, 4 * 3)

    if debug:
        @block.gpsimd
        def _(e):
            e.wait_ge(s_lnv.sem, NQ * (2 + NKC))
            e.dma_start(out=dbg["d_xnh"][:, :], in_=xnh[:]).then_inc(s_out.sem, 16)
            e.dma_start(out=dbg["d_xnl"][:, :], in_=xnl[:]).then_inc(s_out.sem, 16)
        s_out.n += 32

    free(cm_psx, cm_psxx, cm_prb, cm_pmrb)
    free(cm_xf, cm_sqb, cm_lnhl, cm_statf, cm_xnf)

    wkf, cm_wkf = sb("wkf", [128, 1024], F32)
    wkh = Pair("wkh", [128, NKC * 128], BF16); cm_wkh = wkh
    wkl = Pair("wkl", [128, NKC * 128], BF16); cm_wkl = wkl
    kvf = Pair("kvf", [128, 1024], F32); cm_kvf = kvf
    ktt, cm_ktt = sb("ktt", [128, NKC * 128], F32)
    vpt, cm_vpt = sb("vpt", [128, NKC * 4 * E], F32)
    ps_kv, cm_pkv = psum("ps_kv", [128, 1024])
    ps_tp0, cm_pt0 = psum("ps_tp0", [128, 512])
    ps_tp1, cm_pt1 = psum("ps_tp1", [128, 512])
    ps_vk0, cm_pv0 = psum("ps_vk0", [32, 512])

    def wk_src(grp, kc, which):
        base = C + which * C + grp * 128
        return wqT[kc * 128:(kc + 1) * 128, base:base + 128]

    @block.sync
    def _(e):
        e.wait_ge(s_lnv.sem, 4 * (2 + NKC))
        for grp in range(NGRP):
            for which in range(2):
                iw = 2 * grp + which
                base = C + which * C + grp * 128
                for hf in range(2):
                    if iw or hf:
                        e.wait_ge(s_wkvs.sem, 2 * iw + hf)
                    e.dma_start(out=wkf[:].rearrange(
                        "p (kc q) -> p kc q", kc=8),
                        in_=wqT[hf * 1024:(hf + 1) * 1024,
                                base:base + 128].rearrange(
                            "(kc p) q -> p kc q", p=128)
                        ).then_inc(s_wkv[iw % 2].sem, 16)
            tern.pump_sync(e, 5)
    s_wkv[0].n = NGRP * 32
    s_wkv[1].n = NGRP * 32

    @block.gpsimd
    def _(e):
        e.wait_ge(s_lnv.sem, 4 * (2 + NKC))
        for grp in range(NGRP):
            for which in range(2):
                iw = 2 * grp + which
                if iw >= 2:
                    e.wait_ge(s_kvmm.sem, 2 * (iw - 1))
                base = C + which * C + grp * 128
                e.dma_start(out=wkh[iw % 2][:].rearrange(
                    "p (kc q) -> p kc q", kc=NKC),
                    in_=wqT[:, base:base + 128].rearrange(
                        "(kc p) q -> p kc q", p=128)
                    ).then_inc(s_wkvb[iw % 2].sem, 16)
            tern.pump_gpsimd(e, 5)
    s_wkvb[0].n = NGRP * 16
    s_wkvb[1].n = NGRP * 16

    @block.vector
    def _(e):
        em = Emit(e)

        def split_iw(iw):
            e.wait_ge(s_wkvb[iw % 2].sem, 16 * (iw // 2 + 1))
            if iw >= 2:
                e.wait_ge(s_kvmm.sem, 2 * (iw - 1))
            for hf in range(2):
                e.wait_ge(s_wkv[iw % 2].sem, 16 * (2 * (iw // 2) + hf + 1))
                em.tt(wkl[iw % 2][:, hf * 1024:(hf + 1) * 1024], wkf[:],
                      wkh[iw % 2][:, hf * 1024:(hf + 1) * 1024],
                      ALU.subtract).then_inc(s_wkvs.sem, 1)

        split_iw(0)
        for grp in range(NGRP):
            for which in range(2):
                iw = 2 * grp + which
                if which == 1:
                    split_iw(iw)
                for tt in range(NKC):
                    e.wait_ge(s_tpmm.sem, iw * NKC + tt + 1)
                    pst = [ps_tp0, ps_tp1][tt % 2]
                    if which == 0:
                        em.copy(ktt[:, tt * 128:(tt + 1) * 128],
                                pst[:, 0:128]).then_inc(s_tpe.sem, 1)
                    else:
                        dst = vpt[:, tt * 4 * E:(tt + 1) * 4 * E].rearrange(
                            "p (j x) -> p j x", j=4)
                        em.copy(dst[:, :, 0:D],
                                pst[:, 0:128].rearrange("p (j d) -> p j d", j=4))
                        em.memset(dst[:, :, D:D + 1], 1.0).then_inc(s_tpe.sem, 1)
            e.wait_ge(s_vkmm.sem, grp + 1)
            ps = ps_vk0
            for j in range(4):
                em.copy(bd[32 * j:32 * j + 32,
                           grp * 128 + 32 * j: grp * 128 + 32 * j + 32],
                        ps[0:32, E * j:E * j + D])
                ins = em.copy(ncol[32 * j:32 * j + 32, grp * 4 + j:grp * 4 + j + 1],
                              ps[0:32, E * j + D:E * j + E])
            ins.then_inc(s_vke.sem, 1)
            tern.pump_vector(e, em, 5)
            if grp < NGRP - 1:
                split_iw(2 * grp + 2)
    s_wkvs.n = NGRP * 4
    s_tpe.n = NGRP * 2 * NKC
    s_vke.n = NGRP

    @block.tensor
    def _(e):
        for grp in range(NGRP):
            for which in range(2):
                iw = 2 * grp + which
                e.wait_ge(s_wkvs.sem, 2 * (iw + 1))
                for th in range(2):
                    if grp or which or th:
                        e.wait_ge(s_kve.sem, 2 * iw + th)
                    for kc in range(NKC):
                        wh = wkh[iw % 2][:, kc * 128:(kc + 1) * 128]
                        wl = wkl[iw % 2][:, kc * 128:(kc + 1) * 128]
                        first = (kc == 0)
                        last = (kc == NKC - 1)
                        for pi, (wo, xlo) in enumerate([(wh, 0), (wh, 1), (wl, 0)]):
                            xsrc = [(xnh, xnl), (xnph, xnpl)][th][xlo]
                            for s in range(2):
                                ins = e.matmul(ps_kv[:, s * 512:(s + 1) * 512], wo,
                                         xsrc[:, kc * OWN + s * 512:
                                              kc * OWN + (s + 1) * 512],
                                         start=(first and pi == 0),
                                         stop=(last and pi == 2))
                    ins.then_inc(s_kvmm.sem, 1)
                for tt in range(NKC):
                    pst = [ps_tp0, ps_tp1][tt % 2]
                    e.wait_ge(s_kve.sem, 2 * iw + (tt // 8) + 1)
                    if tt >= 2:
                        e.wait_ge(s_tpe.sem, iw * NKC + tt - 1)
                    e.transpose(pst[:, 0:128],
                                kvf[tt // 8][:, (tt % 8) * 128:(tt % 8) * 128 + 128],
                                idn[:]).then_inc(s_tpmm.sem, 1)
            ps = ps_vk0
            e.wait_ge(s_tpe.sem, (grp * 2 + 2) * NKC)
            if grp >= 1:
                e.wait_ge(s_vke.sem, grp)
            e.matmul(ps[0:32, 0:4 * E], zt[:, 0:32], zt[:, 0:4 * E],
                     start=True, stop=False)
            for tt in range(NKC):
                for j in range(4):
                    e.matmul(ps[0:32, E * j:E * j + E],
                             ktt[:, tt * 128 + 32 * j: tt * 128 + 32 * j + 32],
                             vpt[:, tt * 4 * E + j * E: tt * 4 * E + (j + 1) * E],
                             start=False, stop=False)
            e.matmul(ps[0:32, 0:4 * E], zt[:, 0:32], zt[:, 0:4 * E],
                     start=False, stop=True).then_inc(s_vkmm.sem, 1)
            tern.pump_tensor(e, 5)
    s_kvmm.n = NGRP * 4
    s_tpmm.n = NGRP * 2 * NKC
    s_vkmm.n = NGRP

    @block.scalar
    def _(e):
        em = Emit(e)
        for grp in range(NGRP):
            for which in range(2):
                iw = 2 * grp + which
                for th in range(2):
                    e.wait_ge(s_kvmm.sem, 2 * iw + th + 1)
                    if grp or which:
                        e.wait_ge(s_tpmm.sem, (iw - 1) * NKC + 8 * (th + 1))
                    if which == 0:
                        em.act(kvf[th][:, :], ps_kv[:],
                               ACT.Relu).then_inc(s_kve.sem, 1)
                    else:
                        em.copy(kvf[th][:, :], ps_kv[:]).then_inc(s_kve.sem, 1)
            tern.pump_scalar(e, em, 5)
    s_kve.n = NGRP * 4

    if debug:
        @block.gpsimd
        def _(e):
            e.wait_ge(s_vke.sem, NGRP)
            e.dma_start(out=dbg["d_bd"][:, :], in_=bd[:]).then_inc(s_out.sem, 16)
            e.dma_start(out=dbg["d_ncol"][:, :], in_=ncol[:]).then_inc(s_out.sem, 16)
        s_out.n += 32

    free(cm_pkv, cm_pt0, cm_pt1, cm_pv0)
    free(cm_xnph, cm_xnpl, cm_wkf, cm_wkh, cm_wkl, cm_kvf, cm_ktt, cm_vpt)

    for tb in range(NTB):
        t0 = tb * TBW
        prev_out_n = s_out.n
        prev_you = (s_you[0].n, s_you[1].n)

        atth, cm_ath = sb(f"atth{tb}", [128, NKC * TBW], BF16)
        attl, cm_atl = sb(f"attl{tb}", [128, NKC * TBW], BF16)
        qsb, cm_qsb = sb(f"qsb{tb}", [128, NKC * TBW], F32)
        wqf, cm_wqf = sb(f"wqf{tb}", [128, 2048], F32)
        wqh = Pair(f"wqh{tb}", [128, 2048], BF16); cm_wqh = wqh
        wql = Pair(f"wql{tb}", [128, 2048], BF16); cm_wql = wql
        ps_q0, cm_pq0 = psum(f"ps_q0_{tb}", [128, TBW])
        ps_q1, cm_pq1 = psum(f"ps_q1_{tb}", [128, TBW])

        nwq0, nwqb0, nwqs0, nqmm0, nqe0 = (s_wq[0].n, s_wqb[0].n, s_wqs.n,
                                           s_qmm.n, s_qe.n)

        def entry_wait(e, tb=tb, prev_out_n=prev_out_n, prev_you=prev_you):
            if tb == 0:
                e.wait_ge(s_vke.sem, NGRP)
                e.wait_ge(s_vkmm.sem, NGRP)
            else:
                e.wait_ge(s_you[0].sem, prev_you[0])
                e.wait_ge(s_you[1].sem, prev_you[1])
                if prev_out_n:
                    e.wait_ge(s_out.sem, prev_out_n)

        @block.sync
        def _(e, nwqs0=nwqs0, nqmm0=nqmm0, entry_wait=entry_wait):
            entry_wait(e)
            for m in range(NKC):
                if m >= 1:
                    e.wait_ge(s_wqs.sem, nwqs0 + m)
                e.dma_start(out=wqf[:].rearrange(
                    "p (kc q) -> p kc q", kc=NKC),
                    in_=wqT[:, m * 128:(m + 1) * 128].rearrange(
                        "(kc p) q -> p kc q", p=128)
                    ).then_inc(s_wq[m % 2].sem, 16)
                if tb == 0 and m % 4 == 3:
                    tern.pump_sync(e, 4)
        s_wq[0].n += NKC * 8
        s_wq[1].n += NKC * 8

        @block.gpsimd
        def _(e, nwqs0=nwqs0, nqmm0=nqmm0, entry_wait=entry_wait):
            entry_wait(e)
            for m in range(NKC):
                if m >= 2:
                    e.wait_ge(s_qmm.sem, nqmm0 + (m - 1) * NKC)
                e.dma_start(out=wqh[m % 2][:].rearrange(
                    "p (kc q) -> p kc q", kc=NKC),
                    in_=wqT[:, m * 128:(m + 1) * 128].rearrange(
                        "(kc p) q -> p kc q", p=128)
                    ).then_inc(s_wqb[m % 2].sem, 16)
                if tb == 0 and m % 4 == 3:
                    tern.pump_gpsimd(e, 4)
        s_wqb[0].n += NKC * 8
        s_wqb[1].n += NKC * 8

        @block.vector
        def _(e, nwq0=nwq0, nwqb0=nwqb0, nqmm0=nqmm0, entry_wait=entry_wait):
            em = Emit(e)
            entry_wait(e)
            for m in range(NKC):
                e.wait_ge(s_wq[m % 2].sem, nwq0 + 16 * (m // 2 + 1))
                e.wait_ge(s_wqb[m % 2].sem, nwqb0 + 16 * (m // 2 + 1))
                if m >= 2:
                    e.wait_ge(s_qmm.sem, nqmm0 + (m - 1) * NKC)
                em.tt(wql[m % 2], wqf[:], wqh[m % 2],
                      ALU.subtract).then_inc(s_wqs.sem, 1)
                if tb == 0 and m % 4 == 3:
                    tern.pump_vector(e, em, 4)
        s_wqs.n += NKC

        @block.tensor
        def _(e, t0=t0, nwqs0=nwqs0, nqe0=nqe0, entry_wait=entry_wait):
            entry_wait(e)
            for i in range(NKC * NKC):
                m, kc = divmod(i, NKC)
                psq = [ps_q0, ps_q1][m % 2]
                if kc == 0 and m >= 2:
                    e.wait_ge(s_qe.sem, nqe0 + m - 1)
                if kc == 0:
                    e.wait_ge(s_wqs.sem, nwqs0 + m + 1)
                b = m % 2
                xh = xnh[:, kc * OWN + t0: kc * OWN + t0 + TBW]
                xl = xnl[:, kc * OWN + t0: kc * OWN + t0 + TBW]
                wh_ = wqh[b][:, kc * 128:(kc + 1) * 128]
                wl_ = wql[b][:, kc * 128:(kc + 1) * 128]
                e.matmul(psq[:], wh_, xh, start=(kc == 0), stop=False)
                e.matmul(psq[:], wl_, xh, start=False, stop=False)
                e.matmul(psq[:], wh_, xl, start=False,
                         stop=(kc == NKC - 1)).then_inc(s_qmm.sem, 1)
                if tb == 0 and kc == NKC - 1 and m % 4 == 3:
                    tern.pump_tensor(e, 4)
        s_qmm.n += NKC * NKC

        @block.scalar
        def _(e, nqmm0=nqmm0):
            em = Emit(e)
            for m in range(NKC):
                e.wait_ge(s_qmm.sem, nqmm0 + NKC * (m + 1))
                psq = [ps_q0, ps_q1][m % 2]
                em.act(qsb[:, m * TBW:(m + 1) * TBW], psq[:],
                       ACT.Relu).then_inc(s_qe.sem, 1)
                if tb == 0 and m % 4 == 3:
                    tern.pump_scalar(e, em, 4)
        s_qe.n += NKC

        if debug and tb == 0:
            @block.gpsimd
            def _(e, nqe0=nqe0):
                e.wait_ge(s_qe.sem, nqe0 + NKC)
                e.dma_start(out=dbg["d_q"][:, :], in_=qsb[:]).then_inc(s_out.sem, 16)
            s_out.n += 16

        nrm, cm_nrm = sb(f"nrm{tb}", [4, TBW], F32)
        nrh, cm_nrh = sb(f"nrh{tb}", [4, TBW], BF16)
        nrl, cm_nrl = sb(f"nrl{tb}", [4, TBW], BF16)
        rbs, cm_rbs = sb(f"rbs{tb}", [128, TBW], F32)
        attf, cm_attf = sb(f"attf{tb}", [128, TBW], F32)
        ps_mn, cm_pmn = psum(f"ps_mn{tb}", [128, TBW])
        ps_nm, cm_pnm = psum(f"ps_nm{tb}", [4, TBW])
        ps_rc, cm_prc = psum(f"ps_rc{tb}", [128, TBW])

        noe0, nov0 = s_oemm.n, s_oev.n

        @block.tensor
        def _(e, noe0=noe0, nov0=nov0, nqe0=nqe0):
            for g in range(NGRP):
                e.wait_ge(s_qe.sem, nqe0 + g + 1)
                if g >= 1:
                    e.wait_ge(s_oev.sem, nov0 + 2 * g)
                q = qsb[:, g * TBW:(g + 1) * TBW]
                e.matmul(ps_nm[0:4, :], ncol[:, g * 4:(g + 1) * 4], q,
                         start=True, stop=True).then_inc(s_oemm.sem, 1)
                e.matmul(ps_mn[:], bd[:, g * 128:(g + 1) * 128], q,
                         start=True, stop=True)
                e.wait_ge(s_oev.sem, nov0 + 2 * g + 1)
                e.matmul(ps_rc[:], ind4[:], nrh[:], start=True, stop=False)
                e.matmul(ps_rc[:], ind4[:], nrl[:], start=False,
                         stop=True).then_inc(s_oemm.sem, 1)
                if tb == 0 and g % 4 == 3:
                    tern.pump_tensor(e, {3: 5, 7: 5, 11: 5, 15: 4}[g])
        s_oemm.n += NGRP * 2

        @block.vector
        def _(e, noe0=noe0):
            em = Emit(e)
            for g in range(NGRP):
                e.wait_ge(s_oemm.sem, noe0 + 2 * g + 1)
                em.ts(nrm[:], ps_nm[:], ATT_EPS, ALU.add)
                em.recip(nrm[:], nrm[:])
                em.copy(nrh[:], nrm[:])
                em.tt(nrl[:], nrm[:], nrh[:], ALU.subtract).then_inc(s_oev.sem, 1)
                e.wait_ge(s_oemm.sem, noe0 + 2 * g + 2)
                em.copy(rbs[:], ps_rc[:])
                em.tt(attf[:], ps_mn[:], rbs[:], ALU.mult)
                em.copy(atth[:, g * TBW:(g + 1) * TBW], attf[:])
                em.tt(attl[:, g * TBW:(g + 1) * TBW], attf[:],
                      atth[:, g * TBW:(g + 1) * TBW],
                      ALU.subtract).then_inc(s_oev.sem, 1)
                if tb == 0 and g % 4 == 3:
                    tern.pump_vector(e, em, {3: 5, 7: 5, 11: 4, 15: 4}[g])
        s_oev.n += NGRP * 2

        if tb == 0:
            @block.scalar
            def _(e):
                em = Emit(e)
                tern.pump_scalar(e, em, 19)

            @block.sync
            def _(e):
                tern.pump_sync(e, 20)

            @block.gpsimd
            def _(e):
                tern.pump_gpsimd(e, 17)
            assert tern.ig == NT_T and tern.iv == NT_T \
                and tern.ipe == NT_T and tern.isc == NT_T \
                and tern.iwb == NT_T, \
                (tern.ig, tern.iv, tern.ipe, tern.isc, tern.iwb)

        if debug and tb == 0:
            @block.gpsimd
            def _(e, nov0=nov0):
                e.wait_ge(s_oev.sem, nov0 + 2 * NGRP)
                e.dma_start(out=dbg["d_atth"][:, :],
                            in_=atth[:]).then_inc(s_out.sem, 16)
            s_out.n += 16

        free(cm_pmn, cm_pnm, cm_prc)
        free(cm_pq0, cm_pq1)
        free(cm_qsb, cm_wqf, cm_wqh, cm_wql, cm_nrm, cm_nrh, cm_nrl,
             cm_rbs, cm_attf)

        nout_dbg = s_out.n
        ffnr, cm_ffnr = sb(f"ffnr{tb}", [128, NKC * TBW], F32)
        gh, cm_gh = sb(f"gh{tb}", [128, (HS // 128) * TBW], BF16)
        gl, cm_gl = sb(f"gl{tb}", [128, (HS // 128) * TBW], BF16)
        w1r = Pair(f"w1r{tb}", [128, 16 * 128], BF16); cm_w1r = w1r
        w2r = Pair(f"w2r{tb}", [128, 2048], BF16); cm_w2r = w2r
        grf, cm_grf = sb(f"grf{tb}", [128, TBW], F32)
        ggf = Pair(f"ggf{tb}", [128, TBW], F32); cm_ggf = ggf
        gg2 = Ring(f"gg2{tb}", [128, TBW], F32, 2); cm_gg2 = gg2
        msv, cm_msv = sb(f"msv{tb}", [1, TBW], F32)
        ps_rs, cm_prs = psum(f"ps_rs{tb}", [128, TBW])
        ps_fc = []
        cm_fcs = []
        for z in range(2):
            t, cm = psum(f"ps_fc{z}_{tb}", [128, TBW])
            ps_fc.append(t)
            cm_fcs.append(cm)
        ps_fp0, cm_pfp0 = psum(f"ps_fp0_{tb}", [128, TBW])
        ps_fp1, cm_pfp1 = psum(f"ps_fp1_{tb}", [128, TBW])

        nw1r0 = s_w1r[0].n
        nfc0 = s_fcmm.n
        nfce0 = s_fce.n
        ngle0 = s_gle.n
        nw2r0 = s_w2r[0].n
        nfp0 = s_fpmm.n
        nfpe0 = s_fpe.n
        nms0 = s_msv.n
        nrsb0 = s_rsb.n

        @block.vector
        def _(e, nout_dbg=nout_dbg):
            em = Emit(e)
            if debug:
                e.wait_ge(s_out.sem, nout_dbg)
            em.memset(ffnr[:], 0.0)
            e.drain()

        @block.sync
        def _(e, nw1r0=nw1r0, nfc0=nfc0, nov0=nov0, nout_dbg=nout_dbg):
            for s4 in s_tw:
                e.wait_ge(s4.sem, (NT_T // 2) * 16)
            e.wait_ge(s_oev.sem, nov0 + 2 * NGRP)
            if debug:
                e.wait_ge(s_out.sem, nout_dbg)
            for m in range(NHC):
                if m >= 2:
                    e.wait_ge(s_fcmm.sem, nfc0 + m - 1)
                e.dma_start(out=w1r[m % 2],
                            in_=w1s[m, :, :, :].rearrange("t p q -> p t q")
                            ).then_inc(s_w1r[m % 2].sem, 16)
        s_w1r[0].n += (NHC // 2) * 16
        s_w1r[1].n += (NHC // 2) * 16

        @block.tensor
        def _(e, t0=t0, nw1r0=nw1r0, nfce0=nfce0, ngle0=ngle0, nw2r0=nw2r0,
              nfpe0=nfpe0, nrsb0=nrsb0, nms0=nms0):
            ip = 0
            for hs in range(NHS):
                for mloc in range(8):
                    m = hs * 8 + mloc
                    e.wait_ge(s_w1r[m % 2].sem, nw1r0 + 16 * (m // 2 + 1))
                    if m >= 2:
                        e.wait_ge(s_fce.sem, nfce0 + m - 1)
                    for kc in range(NKC):
                        xhk = xnh[:, kc * OWN + t0: kc * OWN + t0 + TBW]
                        xlk = xnl[:, kc * OWN + t0: kc * OWN + t0 + TBW]
                        w = w1r[m % 2][:, kc * 128:(kc + 1) * 128]
                        e.matmul(ps_fc[m % 2][:], w, xhk, start=(kc == 0), stop=False)
                        ins = e.matmul(ps_fc[m % 2][:], w, xlk, start=False,
                                       stop=(kc == NKC - 1))
                    ins.then_inc(s_fcmm.sem, 1)
                    if m >= 1:
                        e.wait_ge(s_fce.sem, nfce0 + m)
                        e.matmul(ps_rs[0:1, :], onesf[:], gg2[(m - 1) % 2],
                                 start=(m == 1),
                                 stop=False).then_inc(s_msv.sem, 1)
                e.wait_ge(s_gle.sem, ngle0 + 8 * (hs + 1))
                for mcp in range(8):
                    if hs or mcp:
                        e.wait_ge(s_fpe.sem, nfpe0 + 2 * (hs * 8 + mcp - 1) + 2)
                    iblk = hs * 8 + mcp
                    e.wait_ge(s_w2r[iblk % 2].sem, nw2r0 + 32 * (iblk // 2 + 1))
                    b2 = iblk % 2
                    for khl in range(8):
                        for half2 in range(2):
                            w = w2r[b2][:, khl * 256 + half2 * 128:
                                        khl * 256 + (half2 + 1) * 128]
                            gsl = slice(khl * TBW, (khl + 1) * TBW)
                            psf = [ps_fp0, ps_fp1][half2]
                            e.matmul(psf[:], w, gh[:, gsl],
                                     start=(khl == 0), stop=False)
                            ins = e.matmul(psf[:], w, gl[:, gsl], start=False,
                                           stop=(khl == 7))
                        ins.then_inc(s_fpmm.sem, 1)
                    ip += 1
            e.wait_ge(s_fce.sem, nfce0 + NHC)
            e.matmul(ps_rs[0:1, :], onesf[:], gg2[(NHC - 1) % 2],
                     start=False, stop=True).then_inc(s_msv.sem, 1)
            e.wait_ge(s_rsb.sem, nrsb0 + 3)
            e.matmul(ps_rs[:], ones1[:], rstd_b[0:1, t0:t0 + TBW],
                     start=True, stop=False)
            e.matmul(ps_rs[:], ones1[:], rstd_b[0:1, OWN + t0:OWN + t0 + TBW],
                     start=False, stop=True).then_inc(s_rsb.sem, 1)
        s_fcmm.n += NHC
        s_msv.n += NHC
        s_fpmm.n += NHS * 8 * 8
        s_rsb.n += 1

        @block.scalar
        def _(e, nfc0=nfc0, ngle0=ngle0, nms0=nms0, nfp0=nfp0, nrsb0=nrsb0,
              nov0=nov0, nout_dbg=nout_dbg):
            em = Emit(e)
            e.wait_ge(s_oev.sem, nov0 + 2 * NGRP)
            if debug:
                e.wait_ge(s_out.sem, nout_dbg)
            ip = 0
            for hs in range(NHS):
                for mloc in range(8):
                    m = hs * 8 + mloc
                    e.wait_ge(s_fcmm.sem, nfc0 + m + 1)
                    if m >= 2:
                        e.wait_ge(s_gle.sem, ngle0 + m - 1)
                        e.wait_ge(s_msv.sem, nms0 + m - 1)
                    em.act(grf[:], ps_fc[m % 2][:], ACT.Relu)
                    em.act(ggf[m % 2], grf[:], ACT.Square)
                    em.act(gh[:, mloc * TBW:(mloc + 1) * TBW], grf[:], ACT.Square)
                    em.act(gg2[m % 2], ggf[m % 2],
                           ACT.Square).then_inc(s_fce.sem, 1)
                for mcp in range(8):
                    if ip >= 2:
                        e.wait_ge(s_fpmm.sem, nfp0 + (ip - 1) * 8)
                    for t_ in range(2):
                        e.dma_start(
                            out=w2r[ip % 2][:].rearrange(
                                "p (k t q) -> p k t q", k=8, t=2)[:, :, t_, :],
                            in_=w2s[hs * 8:(hs + 1) * 8, mcp * 2 + t_, :, :]
                            .rearrange("k p q -> p k q")
                            ).then_inc(s_w2r[ip % 2].sem, 16)
                    ip += 1
            e.wait_ge(s_rsb.sem, nrsb0 + 1)
            em.act(msv[:], msv[:], ACT.Sqrt, bias=epst[0:1, 1:2]).then_inc(s_rsb.sem, 1)
        s_fce.n += NHC
        s_w2r[0].n += NHS * 4 * 32
        s_w2r[1].n += NHS * 4 * 32
        s_rsb.n += 1

        @block.vector
        def _(e, t0=t0, nfce0=nfce0, nfp0=nfp0, nfpe0=nfpe0, nms0=nms0,
              nrsb0=nrsb0):
            em = Emit(e)
            for hs in range(NHS):
                for mloc in range(8):
                    m = hs * 8 + mloc
                    e.wait_ge(s_fce.sem, nfce0 + m + 1)
                    em.tt(gl[:, mloc * TBW:(mloc + 1) * TBW], ggf[m % 2],
                          gh[:, mloc * TBW:(mloc + 1) * TBW],
                          ALU.subtract).then_inc(s_gle.sem, 1)
                for mcp in range(8):
                    e.wait_ge(s_fpmm.sem, nfp0 + (hs * 8 + mcp) * 8 + 8)
                    for half2 in range(2):
                        mc = mcp * 2 + half2
                        psf = [ps_fp0, ps_fp1][half2]
                        em.tt(ffnr[:, mc * TBW:(mc + 1) * TBW],
                              ffnr[:, mc * TBW:(mc + 1) * TBW], psf[:],
                              ALU.add).then_inc(s_fpe.sem, 1)
            e.wait_ge(s_msv.sem, nms0 + NHC)
            em.ts(msv[:], ps_rs[0:1, :], INV_HID, ALU.mult).then_inc(s_rsb.sem, 1)
            e.wait_ge(s_rsb.sem, nrsb0 + 2)
            em.recip(msv[:], msv[:])
            em.copy(rstd_b[0:1, t0:t0 + TBW], msv[:])
            em.tt(rstd_b[0:1, OWN + t0:OWN + t0 + TBW], msv[:],
                  rstd_b[0:1, t0:t0 + TBW], ALU.subtract).then_inc(s_rsb.sem, 1)
        s_gle.n += NHC
        s_fpe.n += NHS * 16
        s_rsb.n += 2

        if debug and tb == 0:
            @block.gpsimd
            def _(e, nfpe0=nfpe0):
                e.wait_ge(s_fpe.sem, nfpe0 + NHS * 16)
                e.dma_start(out=dbg["d_ffnr"][:, :],
                            in_=ffnr[:]).then_inc(s_out.sem, 16)
            s_out.n += 16

        free(*cm_fcs, cm_pfp0, cm_pfp1)
        free(cm_gh, cm_gl, cm_w1r, cm_w2r, cm_grf, cm_ggf, cm_gg2, cm_msv)

        wpf, cm_wpf = sb(f"wpf{tb}", [128, 2048], F32)
        wph = Pair(f"wph{tb}", [128, 2048], BF16); cm_wph = wph
        wpl = Pair(f"wpl{tb}", [128, 2048], BF16); cm_wpl = wpl
        xre = Pair(f"xre{tb}", [128, TBW], F32); cm_xre = xre
        fin1, cm_f1 = sb(f"fin1{tb}", [128, TBW], F32)
        outb = Pair(f"outb{tb}", [128, TBW], F32); cm_ob = outb
        ps_p0, cm_pp0b = psum(f"ps_p0_{tb}", [128, TBW])
        ps_p1, cm_pp1b = psum(f"ps_p1_{tb}", [128, TBW])

        nwp0, nwpb0, nwps0, npmm0, nfin0 = (s_wp[0].n, s_wpb[0].n, s_wps.n,
                                            s_pmm.n, s_fin.n)
        nxre0 = s_xre[0].n
        nout0 = s_out.n
        nyou0 = (s_you[0].n, s_you[1].n)

        @block.sync
        def _(e, t0=t0, nwps0=nwps0, npmm0=npmm0, nfin0=nfin0, nfpe0=nfpe0):
            e.wait_ge(s_fpe.sem, nfpe0 + NHS * 16)

            def xre_load(mx):
                if mx >= 2:
                    e.wait_ge(s_fin.sem, nfin0 + mx - 1)
                e.dma_start(out=xre[mx % 2],
                            in_=xT[mx * 128:(mx + 1) * 128, t0:t0 + TBW]
                            ).then_inc(s_xre[mx % 2].sem, 16)

            for m in range(NKC):
                if m >= 2:
                    xre_load(m - 2)
                if m >= 1:
                    e.wait_ge(s_wps.sem, nwps0 + m)
                e.dma_start(out=wpf[:].rearrange(
                    "p (kc q) -> p kc q", kc=NKC),
                    in_=wpT[:, m * 128:(m + 1) * 128].rearrange(
                        "(kc p) q -> p kc q", p=128)
                    ).then_inc(s_wp[m % 2].sem, 16)
            xre_load(NKC - 2)
            xre_load(NKC - 1)
        s_wp[0].n += NKC * 8
        s_wp[1].n += NKC * 8
        s_xre[0].n += NKC * 8
        s_xre[1].n += NKC * 8

        @block.gpsimd
        def _(e, t0=t0, nwps0=nwps0, npmm0=npmm0, nfpe0=nfpe0, nfin0=nfin0):
            e.wait_ge(s_fpe.sem, nfpe0 + NHS * 16)

            def y_dma(my):
                e.wait_ge(s_fin.sem, nfin0 + my + 1)
                e.dma_start(out=y[my * 128:(my + 1) * 128, t0:t0 + TBW],
                            in_=outb[my % 2][:]).then_inc(s_you[my % 2].sem, 16)

            for m in range(NKC):
                if m >= 2:
                    y_dma(m - 2)
                    e.wait_ge(s_pmm.sem, npmm0 + (m - 1) * NKC)
                e.dma_start(out=wph[m % 2][:].rearrange(
                    "p (kc q) -> p kc q", kc=NKC),
                    in_=wpT[:, m * 128:(m + 1) * 128].rearrange(
                        "(kc p) q -> p kc q", p=128)
                    ).then_inc(s_wpb[m % 2].sem, 16)
            y_dma(NKC - 2)
            y_dma(NKC - 1)
        s_wpb[0].n += NKC * 8
        s_wpb[1].n += NKC * 8
        s_you[0].n += NKC * 8
        s_you[1].n += NKC * 8

        @block.vector
        def _(e, t0=t0, nwp0=nwp0, nwpb0=nwpb0, npmm0=npmm0, nfin0=nfin0,
              nout0=nout0, nxre0=nxre0, nyou0=nyou0):
            em = Emit(e)

            def final(m):
                e.wait_ge(s_pmm.sem, npmm0 + NKC * (m + 1))
                e.wait_ge(s_xre[m % 2].sem, nxre0 + 16 * (m // 2 + 1))
                if m >= 2:
                    e.wait_ge(s_you[m % 2].sem, nyou0[m % 2] + 16 * (m // 2))
                psp = [ps_p0, ps_p1][m % 2]
                em.tt(fin1[:], ffnr[:, m * TBW:(m + 1) * TBW], ps_rs[:], ALU.mult)
                em.tt(fin1[:], fin1[:], psp[:], ALU.add)
                em.stt(outb[m % 2], fin1[:], bpt[:, m:m + 1], xre[m % 2],
                       ALU.add, ALU.add).then_inc(s_fin.sem, 1)

            for m in range(NKC):
                if m >= 2:
                    final(m - 2)
                e.wait_ge(s_wp[m % 2].sem, nwp0 + 16 * (m // 2 + 1))
                e.wait_ge(s_wpb[m % 2].sem, nwpb0 + 16 * (m // 2 + 1))
                if m >= 2:
                    e.wait_ge(s_pmm.sem, npmm0 + (m - 1) * NKC)
                em.tt(wpl[m % 2], wpf[:], wph[m % 2],
                      ALU.subtract).then_inc(s_wps.sem, 1)
            for m in range(NKC - 2, NKC):
                final(m)
        s_wps.n += NKC
        s_fin.n += NKC

        @block.tensor
        def _(e, nwps0=nwps0, nfin0=nfin0, nrsb0=nrsb0):
            e.wait_ge(s_rsb.sem, nrsb0 + 4)
            for i in range(NKC * NKC):
                m, kc = divmod(i, NKC)
                psp = [ps_p0, ps_p1][m % 2]
                if kc == 0 and m >= 2:
                    e.wait_ge(s_fin.sem, nfin0 + m - 1)
                if kc == 0:
                    e.wait_ge(s_wps.sem, nwps0 + m + 1)
                b = m % 2
                ah = atth[:, kc * TBW:(kc + 1) * TBW]
                al = attl[:, kc * TBW:(kc + 1) * TBW]
                wh_ = wph[b][:, kc * 128:(kc + 1) * 128]
                wl_ = wpl[b][:, kc * 128:(kc + 1) * 128]
                e.matmul(psp[:], wh_, ah, start=(kc == 0), stop=False)
                e.matmul(psp[:], wl_, ah, start=False, stop=False)
                e.matmul(psp[:], wh_, al, start=False,
                         stop=(kc == NKC - 1)).then_inc(s_pmm.sem, 1)
        s_pmm.n += NKC * NKC



        free(cm_pp0b, cm_pp1b)
        free(cm_prs)
        free(cm_ath, cm_atl, cm_ffnr, cm_wpf, cm_wph, cm_wpl, cm_xre,
             cm_f1, cm_ob)

    @block.gpsimd
    def _(e):
        e.wait_ge(s_you[0].sem, s_you[0].n)
        e.wait_ge(s_you[1].sem, s_you[1].n)
        if s_out.n:
            e.wait_ge(s_out.sem, s_out.n)

    block_cm.__exit__(None, None, None)
    return nc, dbg


_CONSTS = None


def _consts():
    global _CONSTS
    if _CONSTS is None:
        sel = np.zeros((128, 2), ml_dtypes.bfloat16)
        for p in range(128):
            sel[p, p // 64] = ml_dtypes.bfloat16(1.0 / 64.0)
        ind2 = np.zeros((2, 128), ml_dtypes.bfloat16)
        for p in range(128):
            ind2[p // 64, p] = ml_dtypes.bfloat16(1.0)
        ind4 = np.zeros((4, 128), ml_dtypes.bfloat16)
        for p in range(128):
            ind4[p // 32, p] = ml_dtypes.bfloat16(1.0)
        ones1 = np.ones((1, 128), ml_dtypes.bfloat16)
        onesf = np.ones((128, 1), np.float32)
        idn = np.eye(128, dtype=np.float32)
        idnb = np.eye(128, dtype=np.float32).astype(ml_dtypes.bfloat16)
        eps = np.array([[LN_EPS, RMS_EPS]], np.float32)
        _CONSTS = dict(seld=sel, ind2d=ind2, ind4d=ind4, ones1d=ones1,
                       onesfd=onesf, idnd=idn, idnbd=idnb, epsd=eps)
    return _CONSTS


def make_in_maps(x, wq, wp, wf, w2, bp):
    consts = _consts()
    in_maps = []
    for core in range(8):
        b, half = divmod(core, 2)
        xt = np.asarray(x[b].T)
        if half == 0:
            xcore = np.ascontiguousarray(xt)
        else:
            xcore = np.ascontiguousarray(
                np.concatenate([xt[:, OWN:], xt[:, :OWN]], axis=1))
        im = dict(xT=xcore, wqT=wq, wpT=wp, wfn=wf, w2n=w2, bpd=bp)
        im.update(consts)
        in_maps.append(im)
    return in_maps


def kernel(L0_attn_x, L0_attn_w_qkv, L0_attn_w_proj, L0_attn_b_proj,
           L0_ffn_W_fc, L0_ffn_W_proj):
    x = np.asarray(L0_attn_x, dtype=np.float32)
    wq = np.ascontiguousarray(np.asarray(L0_attn_w_qkv, np.float32).T)
    wp = np.ascontiguousarray(np.asarray(L0_attn_w_proj, np.float32).T)
    wf = np.ascontiguousarray(np.asarray(L0_ffn_W_fc, np.float32))
    w2 = np.ascontiguousarray(np.asarray(L0_ffn_W_proj, np.float32))
    bp = np.ascontiguousarray(
        np.asarray(L0_attn_b_proj, np.float32).reshape(NKC, 128).T)

    nc, _ = build_program(debug=False)
    in_maps = make_in_maps(x, wq, wp, wf, w2, bp)
    res = run_bass_kernel_spmd(nc, in_maps, core_ids=list(range(8)))
    out = np.empty((4, TOK, C), np.float32)
    for core in range(8):
        b, half = divmod(core, 2)
        yc = res.results[core]["y"]
        out[b, half * OWN:(half + 1) * OWN, :] = yc.T
    return out


# version 14
# speedup vs baseline: 1.0191x; 1.0072x over previous
import numpy as np
import ml_dtypes
import concourse.bass as bass
import concourse.mybir as mybir
from concourse.bass_utils import run_bass_kernel_spmd

F32 = mybir.dt.float32
BF16 = mybir.dt.bfloat16
ALU = mybir.AluOpType
ACT = mybir.ActivationFunctionType

C = 2048
NKC = 16
TOK = 2048
OWN = 1024
TBW = 512
NTB = OWN // TBW
D = 32
E = 33
NGRP = 16
HID = 8192
NHC = HID // 128
HS = 1024
NHS = HID // HS
THS = 512
LN_EPS = 1e-5
ATT_EPS = 1e-15
RMS_EPS = float(np.finfo(np.float32).eps)
INV_C = 1.0 / C
INV_HID = 1.0 / HID


class Emit:

    def __init__(self, eng):
        self.e = eng
        self.dirty_w = set()
        self.dirty_r = set()

    def _names(self, aps):
        out = set()
        for a in aps:
            if a is not None and hasattr(a, "tensor"):
                out.add(a.tensor.name)
        return out

    def _pre(self, ins, outs):
        rd = self._names(ins)
        wr = self._names(outs)
        if (rd & self.dirty_w) or (wr & (self.dirty_w | self.dirty_r)):
            self.e.drain()
            self.dirty_w = set()
            self.dirty_r = set()
        self.dirty_r |= rd

    def _post(self, outs):
        self.dirty_w |= self._names(outs)

    def copy(self, out, in_):
        self._pre([in_], [out])
        if hasattr(self.e, "tensor_copy"):
            r = self.e.tensor_copy(out, in_)
        else:
            r = self.e.copy(out, in_)
        self._post([out])
        return r

    def ts(self, out, in_, s1, op0, s2=None, op1=None):
        self._pre([in_], [out])
        if op1 is None:
            r = self.e.tensor_scalar(out, in_, s1, None, op0)
        else:
            r = self.e.tensor_scalar(out, in_, s1, s2, op0, op1)
        self._post([out])
        return r

    def tt(self, out, a, b, op):
        self._pre([a, b], [out])
        r = self.e.tensor_tensor(out, a, b, op)
        self._post([out])
        return r

    def stt(self, out, in0, scalar, in1, op0, op1):
        self._pre([in0, in1, scalar if hasattr(scalar, "tensor") else None], [out])
        r = self.e.scalar_tensor_tensor(out, in0, scalar, in1, op0, op1)
        self._post([out])
        return r

    def act(self, out, in_, func, bias=None, scale=None):
        self._pre([in_], [out])
        kw = {}
        if bias is not None:
            kw["bias"] = bias
        if scale is not None:
            kw["scale"] = scale
        r = self.e.activation(out, in_, func, **kw)
        self._post([out])
        return r

    def recip(self, out, in_):
        self._pre([in_], [out])
        r = self.e.reciprocal(out, in_)
        self._post([out])
        return r

    def memset(self, out, val):
        self._pre([], [out])
        r = self.e.memset(out, val)
        self._post([out])
        return r


class Sem:
    def __init__(self, nc, name):
        self.cm = nc.semaphore(name)
        self.sem = self.cm.__enter__()
        self.n = 0


def build_program(debug=False):
    nc = bass.Bass("TRN2", target_bir_lowering=False)

    def sb(name, shape, dt):
        cm = nc.sbuf_tensor(name, shape, dt)
        return cm.__enter__(), cm

    def psum(name, shape, dt=F32):
        cm = nc.psum_tensor(name, shape, dt)
        return cm.__enter__(), cm

    class Ring:
        def __init__(self, name, shape, dt, k):
            self.cms = [nc.sbuf_tensor(f"{name}_{i}", shape, dt) for i in range(k)]
            self.ts = [cm.__enter__() for cm in self.cms]

        def __getitem__(self, i):
            return self.ts[i][:]

        def __exit__(self, *a):
            for cm in reversed(self.cms):
                cm.__exit__(None, None, None)

    class Pair:
        def __init__(self, name, shape, dt):
            self.cms = [nc.sbuf_tensor(f"{name}_{i}", shape, dt) for i in range(2)]
            self.ts = [cm.__enter__() for cm in self.cms]

        def __getitem__(self, i):
            return self.ts[i][:]

        def __exit__(self, *a):
            self.cms[1].__exit__(None, None, None)
            self.cms[0].__exit__(None, None, None)

    def free(*cms):
        for cm in reversed(cms):
            cm.__exit__(None, None, None)

    xT = nc.dram_tensor("xT", [C, TOK], F32, kind="ExternalInput").ap()
    wqT = nc.dram_tensor("wqT", [C, 3 * C], F32, kind="ExternalInput").ap()
    wpT = nc.dram_tensor("wpT", [C, C], F32, kind="ExternalInput").ap()
    wfn = nc.dram_tensor("wfn", [HID, C], F32, kind="ExternalInput").ap()
    w2n = nc.dram_tensor("w2n", [C, HID], F32, kind="ExternalInput").ap()
    bpd = nc.dram_tensor("bpd", [128, NKC], F32, kind="ExternalInput").ap()
    seld = nc.dram_tensor("seld", [128, 2], BF16, kind="ExternalInput").ap()
    ind2d = nc.dram_tensor("ind2d", [2, 128], BF16, kind="ExternalInput").ap()
    ind4d = nc.dram_tensor("ind4d", [4, 128], BF16, kind="ExternalInput").ap()
    ones1d = nc.dram_tensor("ones1d", [1, 128], BF16, kind="ExternalInput").ap()
    onesfd = nc.dram_tensor("onesfd", [128, 1], F32, kind="ExternalInput").ap()
    idnd = nc.dram_tensor("idnd", [128, 128], F32, kind="ExternalInput").ap()
    idnbd = nc.dram_tensor("idnbd", [128, 128], BF16, kind="ExternalInput").ap()
    epsd = nc.dram_tensor("epsd", [1, 2], F32, kind="ExternalInput").ap()
    y = nc.dram_tensor("y", [C, OWN], F32, kind="ExternalOutput").ap()
    w1s = nc.dram_tensor("w1s", [NHC, NKC, 128, 128], BF16, kind="Internal").ap()
    w2s = nc.dram_tensor("w2s", [NHC, NKC, 128, 128], BF16, kind="Internal").ap()
    dbg = {}
    if debug:
        for nm, sh, dt in [
            ("d_xnh", [128, NKC * OWN], BF16), ("d_xnl", [128, NKC * OWN], BF16),
            ("d_bd", [128, NGRP * 128], F32), ("d_ncol", [128, NGRP * 4], F32),
            ("d_q", [128, NKC * TBW], F32), ("d_atth", [128, NKC * TBW], BF16),
            ("d_ffnr", [128, NKC * TBW], F32),
        ]:
            dbg[nm] = nc.dram_tensor(nm, sh, dt, kind="ExternalOutput").ap()

    sel, _ = sb("sel", [128, 2], BF16)
    ind2, _ = sb("ind2", [2, 128], BF16)
    ind4, _ = sb("ind4", [4, 128], BF16)
    ones1, _ = sb("ones1", [1, 128], BF16)
    onesf, _ = sb("onesf", [128, 1], F32)
    idn, _ = sb("idn", [128, 128], F32)
    idnb, _ = sb("idnb", [128, 128], BF16)
    zt, _ = sb("zt", [128, 160], BF16)
    epst, _ = sb("epst", [1, 2], F32)
    bpt, _ = sb("bpt", [128, NKC], F32)
    xnh, _ = sb("xnh", [128, NKC * OWN], BF16)
    xnl, _ = sb("xnl", [128, NKC * OWN], BF16)
    bd, _ = sb("bd", [128, NGRP * 128], F32)
    ncol, _ = sb("ncol", [128, NGRP * 4], F32)
    rstd_b, _ = sb("rstd_b", [1, 2 * OWN], BF16)

    def S(name):
        return Sem(nc, name)

    s_cst = S("s_cst")
    s_wfb = [S(f"s_wfb{i}") for i in range(2)]
    s_tg = S("s_tg")
    s_tq = S("s_tq")
    s_tpm = S("s_tpm")
    s_tse = S("s_tse")
    s_tv = S("s_tv")
    s_tw = [S(f"s_tw{i}") for i in range(2)]
    s_ldx = [S("s_ldx0"), S("s_ldx1")]
    s_xnv = S("s_xnv")
    s_xnc = S("s_xnc")
    s_sq = S("s_sq")
    s_sqr = S("s_sqr")
    s_stat = S("s_stat")
    s_lnv = S("s_lnv")
    s_wkv = [S("s_wkv0"), S("s_wkv1")]
    s_wkvb = [S("s_wkvb0"), S("s_wkvb1")]
    s_wkvs = S("s_wkvs")
    s_kvmm = S("s_kvmm")
    s_kve = S("s_kve")
    s_tpmm = S("s_tpmm")
    s_tpe = S("s_tpe")
    s_vkmm = S("s_vkmm")
    s_vke = S("s_vke")
    s_wq = [S("s_wq0"), S("s_wq1")]
    s_wqb = [S("s_wqb0"), S("s_wqb1")]
    s_wqs = S("s_wqs")
    s_qmm = S("s_qmm")
    s_qe = S("s_qe")
    s_oemm = S("s_oemm")
    s_oev = S("s_oev")
    s_w1r = [S("s_w1r0"), S("s_w1r1")]
    s_fcmm = S("s_fcmm")
    s_fce = S("s_fce")
    s_gle = S("s_gle")
    s_w2r = [S("s_w2r0"), S("s_w2r1")]
    s_fpmm = S("s_fpmm")
    s_fpe = S("s_fpe")
    s_msv = S("s_msv")
    s_rsb = S("s_rsb")
    s_wp = [S("s_wp0"), S("s_wp1")]
    s_wpb = [S("s_wpb0"), S("s_wpb1")]
    s_wps = S("s_wps")
    s_pmm = S("s_pmm")
    s_fin = S("s_fin")
    s_xre = [S("s_xre0"), S("s_xre1")]
    s_out = S("s_out")
    s_you = [S("s_you0"), S("s_you1")]

    block_cm = nc.Block()
    block = block_cm.__enter__()

    @block.sync
    def _(e):
        for t, d in [(sel, seld), (ind2, ind2d), (ind4, ind4d), (ones1, ones1d),
                     (onesf, onesfd), (idn, idnd), (idnb, idnbd),
                     (bpt, bpd), (epst, epsd)]:
            e.dma_start(out=t[:], in_=d[:, :]).then_inc(s_cst.sem, 16)
    s_cst.n = 9 * 16

    @block.vector
    def _(e):
        em = Emit(e)
        e.wait_ge(s_cst.sem, s_cst.n)
        em.memset(zt[:], 0.0)
        em.memset(bd[:], 0.0)
        em.memset(ncol[:], 0.0)

    twb = Ring("twb", [128, 2048], BF16, 1); cm_twb = twb
    tred, cm_tred = sb("tred", [128, 32], F32)
    tscl, cm_tscl = sb("tscl", [128, 32], BF16)
    trec, cm_trec = sb("trec", [128, 32], F32)
    trat, cm_trat = sb("trat", [128, 2048], BF16)
    tga, cm_tga = sb("tga", [128, 2048], BF16)
    tw1n, cm_tw1n = sb("tw1n", [128, 2048], BF16)
    tw1t, cm_tw1t = sb("tw1t", [128, 16 * 128], BF16)
    ps_ta, cm_pta = psum("ps_ta", [128, 1024], BF16)
    ps_tb, cm_ptb = psum("ps_tb", [128, 1024], BF16)
    ps_tc, cm_ptc = psum("ps_tc", [128, 1024], BF16)

    tern_blocks = ([("w1", b) for b in range(HID // 128)]
                   + [("w2", rc) for rc in range(64)])
    NT_T = len(tern_blocks)

    def tern_src(blk):
        w, a = blk
        if w == "w1":
            return wfn[a * 128:(a + 1) * 128, :]
        r, c = divmod(a, 4)
        return w2n[r * 128:(r + 1) * 128, c * 2048:(c + 1) * 2048]

    def tern_dst(blk):
        w, a = blk
        if w == "w1":
            return w1s[a, :, :, :]
        r, c = divmod(a, 4)
        return w2s[16 * c:16 * (c + 1), r, :, :]

    class Tern:
        def __init__(self):
            self.ig = 0
            self.iv = 0
            self.ipe = 0
            self.isc = 0
            self.iwb = 0

        def pump_gpsimd(self, e, n):
            em = Emit(e)
            for _ in range(n):
                i = self.ig
                if i >= 1:
                    e.wait_ge(s_tv.sem, 2 * (i - 1) + 1)
                e.dma_start(out=twb[0],
                            in_=tern_src(tern_blocks[i])).then_inc(s_wfb[i % 2].sem, 16)
                self.ig += 1

        def pump_vector(self, e, em, n):
            for _ in range(n):
                i = self.iv
                b = i % 2
                e.wait_ge(s_wfb[b].sem, 16 * (i // 2 + 1))
                e.tensor_reduce(tred[:].rearrange("p (g o) -> p g o", o=1),
                                twb[0].rearrange("p (g o) -> p g o", o=64),
                                mybir.AxisListType.X, ALU.add,
                                apply_absolute_value=True)
                em.dirty_w.add("tred")
                em.ts(tscl[:], tred[:], 1.0 / 64.0, ALU.mult, 1e-8, ALU.max)
                em.recip(trec[:], tscl[:])
                rb = trec[:].rearrange("p (g o) -> p g o", o=1).broadcast_to([128, 32, 64])
                sb_ = tscl[:].rearrange("p (g o) -> p g o", o=1).broadcast_to([128, 32, 64])
                em.tt(trat[:].rearrange("p (g o) -> p g o", o=64),
                      twb[0].rearrange("p (g o) -> p g o", o=64), rb,
                      ALU.mult).then_inc(s_tv.sem, 1)
                em.ts(tga[:], trat[:], 0.5, ALU.is_gt)
                em.stt(tga[:], trat[:], -0.5, tga[:], ALU.is_ge, ALU.add)
                if i >= 1:
                    e.wait_ge(s_tpm.sem, 16 * i)
                em.stt(tw1n[:].rearrange("p (g o) -> p g o", o=64),
                       tga[:].rearrange("p (g o) -> p g o", o=64), -1.0, sb_,
                       ALU.add, ALU.mult).then_inc(s_tv.sem, 1)
                self.iv += 1

        def pump_tensor(self, e, n):
            for _ in range(n):
                i = self.ipe
                e.wait_ge(s_tv.sem, 2 * i + 2)
                for t in range(16):
                    it = 16 * i + t
                    pst = [ps_ta, ps_tb, ps_tc][it % 3]
                    if it >= 3:
                        e.wait_ge(s_tse.sem, it - 2)
                    e.transpose(pst[:, 0:128],
                                tw1n[:, t * 128:(t + 1) * 128],
                                idnb[:]).then_inc(s_tpm.sem, 1)
                self.ipe += 1

        def pump_scalar(self, e, em, n):
            for _ in range(n):
                i = self.isc
                for t in range(16):
                    it = 16 * i + t
                    e.wait_ge(s_tpm.sem, it + 1)
                    if t == 0 and i >= 1:
                        e.wait_ge(s_tw[(i - 1) % 2].sem, 16 * ((i - 1) // 2 + 1))
                    pst = [ps_ta, ps_tb, ps_tc][it % 3]
                    em.copy(tw1t[:, t * 128:(t + 1) * 128],
                            pst[:, 0:128]).then_inc(s_tse.sem, 1)
                self.isc += 1

        def pump_sync(self, e, n):
            for _ in range(n):
                i = self.iwb
                e.wait_ge(s_tse.sem, 16 * (i + 1))
                e.dma_start(out=tern_dst(tern_blocks[i]).rearrange("t p q -> p t q"),
                            in_=tw1t[:]).then_inc(s_tw[i % 2].sem, 16)
                self.iwb += 1

    tern = Tern()
    for j, s4 in enumerate(s_wfb):
        s4.n = (NT_T // 2) * 16
    s_tq.n = NT_T
    s_tv.n = NT_T * 2
    s_tpm.n = NT_T * 16
    s_tse.n = NT_T * 16
    s_tw[0].n = (NT_T // 2) * 16
    s_tw[1].n = (NT_T // 2) * 16

    @block.gpsimd
    def _(e):
        e.wait_ge(s_cst.sem, s_cst.n)
        tern.pump_gpsimd(e, 3)

    @block.vector
    def _(e):
        em = Emit(e)
        tern.pump_vector(e, em, 2)

    @block.tensor
    def _(e):
        e.wait_ge(s_cst.sem, s_cst.n)
        tern.pump_tensor(e, 1)

    @block.scalar
    def _(e):
        em = Emit(e)
        tern.pump_scalar(e, em, 1)

    xnph, cm_xnph = sb("xnph", [128, NKC * OWN], BF16)
    xnpl, cm_xnpl = sb("xnpl", [128, NKC * OWN], BF16)
    xf = Pair("xf", [128, 512], F32); cm_xf = xf
    sqb, cm_sqb = sb("sqb", [128, OWN], F32)
    lnhl, cm_lnhl = sb("lnhl", [1, 2048], BF16)
    statf, cm_statf = sb("statf", [1, 1536], F32)
    xnf = Pair("xnf", [128, 512], F32); cm_xnf = xnf

    def sqh(kc):
        return sqb[:, (kc % 2) * 512:(kc % 2) * 512 + 512]
    stat_m = statf[0:1, 0:512]
    stat_v = statf[0:1, 512:1024]
    stat_t = statf[0:1, 1024:1536]
    ps_sx, cm_psx = psum("ps_sx", [1, 512])
    ps_sxx, cm_psxx = psum("ps_sxx", [1, 512])
    ps_rb, cm_prb = psum("ps_rb", [128, 512])
    ps_mrb, cm_pmrb = psum("ps_mrb", [128, 512])

    NQ = 4

    def q_cols(qi):
        return slice(qi * 512, (qi + 1) * 512)

    @block.sync
    def _(e):
        def wait_consumers(j):
            qi, r = divmod(j, 2 * NKC)
            p, kc = divmod(r, NKC)
            if p == 0:
                e.wait_ge(s_sq.sem, qi * NKC + kc + 1)
                e.wait_ge(s_stat.sem, qi * (NKC + 1) + kc + 1)
            else:
                e.wait_ge(s_lnv.sem, qi * (2 + NKC) + 2 + kc + 1)

        j = 0
        for qi in range(NQ):
            for p in range(2):
                for kc in range(NKC):
                    if j >= 2:
                        wait_consumers(j - 2)
                    e.dma_start(out=xf[j % 2],
                                in_=xT[kc * 128:(kc + 1) * 128, q_cols(qi)]
                                ).then_inc(s_ldx[j % 2].sem, 16)
                    j += 1
            tern.pump_sync(e, 6)
    s_ldx[0].n = NQ * NKC * 16
    s_ldx[1].n = NQ * NKC * 16

    @block.scalar
    def _(e):
        em = Emit(e)
        for qi in range(NQ):
            h, qq = divmod(qi, 2)
            xdh = [xnh, xnph][h]
            c0 = qq * 512
            base = qi * 2 * NKC
            for kc in range(NKC):
                j_ = base + kc
                e.wait_ge(s_ldx[j_ % 2].sem, 16 * (j_ // 2 + 1))
                if kc >= 2:
                    e.wait_ge(s_stat.sem, qi * (NKC + 1) + kc - 1)
                elif qi >= 1:
                    e.wait_ge(s_stat.sem, (qi - 1) * (NKC + 1) + 15 + kc)
                em.act(sqh(kc), xf[j_ % 2],
                       ACT.Square).then_inc(s_sq.sem, 1)
            e.wait_ge(s_lnv.sem, qi * (2 + NKC) + 1)
            em.act(stat_t, stat_v, ACT.Sqrt,
                   bias=epst[0:1, 0:1]).then_inc(s_sqr.sem, 1)
            for kc in range(NKC):
                e.wait_ge(s_xnv.sem, qi * NKC + kc + 1)
                em.copy(xdh[:, kc * OWN + c0:kc * OWN + c0 + 512],
                        xnf[kc % 2]).then_inc(s_xnc.sem, 1)
            tern.pump_scalar(e, em, 6)
    s_sq.n = NQ * NKC
    s_sqr.n = NQ
    s_xnv.n = NQ * NKC
    s_xnc.n = NQ * NKC

    @block.tensor
    def _(e):
        for qi in range(NQ):
            for kc in range(NKC):
                e.wait_ge(s_sq.sem, qi * NKC + kc + 1)
                e.matmul(ps_sx[0:1, :], onesf[:], xf[(qi * 2 * NKC + kc) % 2],
                         start=(kc == 0), stop=(kc == NKC - 1))
                e.matmul(ps_sxx[0:1, :], onesf[:], sqh(kc),
                         start=(kc == 0),
                         stop=(kc == NKC - 1)).then_inc(s_stat.sem, 1)
            e.wait_ge(s_lnv.sem, qi * (2 + NKC) + 2)
            e.matmul(ps_rb[:], ones1[:], lnhl[0:1, 0:512],
                     start=True, stop=False)
            e.matmul(ps_rb[:], ones1[:], lnhl[0:1, 1024:1536],
                     start=False, stop=True)
            e.matmul(ps_mrb[:], ones1[:], lnhl[0:1, 512:1024],
                     start=True, stop=False)
            e.matmul(ps_mrb[:], ones1[:], lnhl[0:1, 1536:2048],
                     start=False, stop=True).then_inc(s_stat.sem, 1)
            tern.pump_tensor(e, 6)
    s_stat.n = NQ * (NKC + 1)



    @block.vector
    def _(e):
        em = Emit(e)
        for qi in range(NQ):
            h, qq = divmod(qi, 2)
            e.wait_ge(s_stat.sem, qi * (NKC + 1) + NKC)
            sm = stat_m
            sv = stat_v
            st = stat_t
            em.ts(sm, ps_sx[:], INV_C, ALU.mult)
            em.ts(sv, ps_sxx[:], INV_C, ALU.mult)
            em.tt(st, sm, sm, ALU.mult)
            em.tt(sv, sv, st, ALU.subtract).then_inc(s_lnv.sem, 1)
            e.wait_ge(s_sqr.sem, qi + 1)
            em.recip(st, st)
            em.tt(sm, sm, st, ALU.mult)
            em.copy(lnhl[0:1, 0:512], st)
            em.tt(lnhl[0:1, 1024:1536], st, lnhl[0:1, 0:512], ALU.subtract)
            em.copy(lnhl[0:1, 512:1024], sm)
            em.tt(lnhl[0:1, 1536:2048], sm, lnhl[0:1, 512:1024],
                  ALU.subtract).then_inc(s_lnv.sem, 1)
            e.wait_ge(s_stat.sem, qi * (NKC + 1) + NKC + 1)
            xdh = [xnh, xnph][h]
            xdl = [xnl, xnpl][h]
            c0 = qq * 512

            def fin_lo(kc):
                e.wait_ge(s_xnc.sem, qi * NKC + kc + 1)
                em.tt(xdl[:, kc * OWN + c0:kc * OWN + c0 + 512],
                      xnf[kc % 2],
                      xdh[:, kc * OWN + c0:kc * OWN + c0 + 512],
                      ALU.subtract).then_inc(s_lnv.sem, 1)

            for kc in range(NKC):
                j_ = (qi * 2 + 1) * NKC + kc
                e.wait_ge(s_ldx[j_ % 2].sem, 16 * (j_ // 2 + 1))
                em.tt(xnf[kc % 2], xf[j_ % 2], ps_rb[:], ALU.mult)
                em.tt(xnf[kc % 2], xnf[kc % 2], ps_mrb[:],
                      ALU.subtract).then_inc(s_xnv.sem, 1)
                if kc >= 1:
                    fin_lo(kc - 1)
            fin_lo(NKC - 1)
            tern.pump_vector(e, em, 6)
    s_lnv.n = NQ * (2 + NKC)

    @block.gpsimd
    def _(e):
        tern.pump_gpsimd(e, 4 * 6)

    if debug:
        @block.gpsimd
        def _(e):
            e.wait_ge(s_lnv.sem, NQ * (2 + NKC))
            e.dma_start(out=dbg["d_xnh"][:, :], in_=xnh[:]).then_inc(s_out.sem, 16)
            e.dma_start(out=dbg["d_xnl"][:, :], in_=xnl[:]).then_inc(s_out.sem, 16)
        s_out.n += 32

    free(cm_psx, cm_psxx, cm_prb, cm_pmrb)
    free(cm_xf, cm_sqb, cm_lnhl, cm_statf, cm_xnf)

    wkf, cm_wkf = sb("wkf", [128, 1024], F32)
    wkh = Pair("wkh", [128, NKC * 128], BF16); cm_wkh = wkh
    wkl = Pair("wkl", [128, NKC * 128], BF16); cm_wkl = wkl
    kvf, cm_kvf = sb("kvf", [128, 1024], F32)
    ktt, cm_ktt = sb("ktt", [128, NKC * 128], F32)
    vpt, cm_vpt = sb("vpt", [128, NKC * 4 * E], F32)
    ps_kv, cm_pkv = psum("ps_kv", [128, 1024])
    ps_tp0, cm_pt0 = psum("ps_tp0", [128, 512])
    ps_tp1, cm_pt1 = psum("ps_tp1", [128, 512])
    ps_vk0, cm_pv0 = psum("ps_vk0", [32, 512])

    def wk_src(grp, kc, which):
        base = C + which * C + grp * 128
        return wqT[kc * 128:(kc + 1) * 128, base:base + 128]

    @block.sync
    def _(e):
        e.wait_ge(s_lnv.sem, 4 * (2 + NKC))
        for grp in range(NGRP):
            for which in range(2):
                iw = 2 * grp + which
                base = C + which * C + grp * 128
                for hf in range(2):
                    if iw or hf:
                        e.wait_ge(s_wkvs.sem, 2 * iw + hf)
                    e.dma_start(out=wkf[:].rearrange(
                        "p (kc q) -> p kc q", kc=8),
                        in_=wqT[hf * 1024:(hf + 1) * 1024,
                                base:base + 128].rearrange(
                            "(kc p) q -> p kc q", p=128)
                        ).then_inc(s_wkv[iw % 2].sem, 16)
            tern.pump_sync(e, 5)
    s_wkv[0].n = NGRP * 32
    s_wkv[1].n = NGRP * 32

    @block.gpsimd
    def _(e):
        e.wait_ge(s_lnv.sem, 4 * (2 + NKC))
        for grp in range(NGRP):
            for which in range(2):
                iw = 2 * grp + which
                if iw >= 2:
                    e.wait_ge(s_kvmm.sem, 2 * (iw - 1))
                base = C + which * C + grp * 128
                e.dma_start(out=wkh[iw % 2][:].rearrange(
                    "p (kc q) -> p kc q", kc=NKC),
                    in_=wqT[:, base:base + 128].rearrange(
                        "(kc p) q -> p kc q", p=128)
                    ).then_inc(s_wkvb[iw % 2].sem, 16)
            tern.pump_gpsimd(e, 5)
    s_wkvb[0].n = NGRP * 16
    s_wkvb[1].n = NGRP * 16

    @block.vector
    def _(e):
        em = Emit(e)
        for grp in range(NGRP):
            for which in range(2):
                iw = 2 * grp + which
                e.wait_ge(s_wkvb[iw % 2].sem, 16 * (iw // 2 + 1))
                if iw >= 2:
                    e.wait_ge(s_kvmm.sem, 2 * (iw - 1))
                for hf in range(2):
                    e.wait_ge(s_wkv[iw % 2].sem, 16 * (2 * (iw // 2) + hf + 1))
                    em.tt(wkl[iw % 2][:, hf * 1024:(hf + 1) * 1024], wkf[:],
                          wkh[iw % 2][:, hf * 1024:(hf + 1) * 1024],
                          ALU.subtract).then_inc(s_wkvs.sem, 1)
                for tt in range(NKC):
                    e.wait_ge(s_tpmm.sem, iw * NKC + tt + 1)
                    pst = [ps_tp0, ps_tp1][tt % 2]
                    if which == 0:
                        em.copy(ktt[:, tt * 128:(tt + 1) * 128],
                                pst[:, 0:128]).then_inc(s_tpe.sem, 1)
                    else:
                        dst = vpt[:, tt * 4 * E:(tt + 1) * 4 * E].rearrange(
                            "p (j x) -> p j x", j=4)
                        em.copy(dst[:, :, 0:D],
                                pst[:, 0:128].rearrange("p (j d) -> p j d", j=4))
                        em.memset(dst[:, :, D:D + 1], 1.0).then_inc(s_tpe.sem, 1)
            e.wait_ge(s_vkmm.sem, grp + 1)
            ps = ps_vk0
            for j in range(4):
                em.copy(bd[32 * j:32 * j + 32,
                           grp * 128 + 32 * j: grp * 128 + 32 * j + 32],
                        ps[0:32, E * j:E * j + D])
                ins = em.copy(ncol[32 * j:32 * j + 32, grp * 4 + j:grp * 4 + j + 1],
                              ps[0:32, E * j + D:E * j + E])
            ins.then_inc(s_vke.sem, 1)
            tern.pump_vector(e, em, 5)
    s_wkvs.n = NGRP * 4
    s_tpe.n = NGRP * 2 * NKC
    s_vke.n = NGRP

    @block.tensor
    def _(e):
        for grp in range(NGRP):
            for which in range(2):
                iw = 2 * grp + which
                for th in range(2):
                    if grp or which or th:
                        e.wait_ge(s_kve.sem, 2 * iw + th)
                    for kc in range(NKC):
                        if kc == 0:
                            e.wait_ge(s_wkvs.sem, 2 * iw + 1)
                        elif kc == 8:
                            e.wait_ge(s_wkvs.sem, 2 * iw + 2)
                        wh = wkh[iw % 2][:, kc * 128:(kc + 1) * 128]
                        wl = wkl[iw % 2][:, kc * 128:(kc + 1) * 128]
                        first = (kc == 0)
                        last = (kc == NKC - 1)
                        for pi, (wo, xlo) in enumerate([(wh, 0), (wh, 1), (wl, 0)]):
                            xsrc = [(xnh, xnl), (xnph, xnpl)][th][xlo]
                            for s in range(2):
                                ins = e.matmul(ps_kv[:, s * 512:(s + 1) * 512], wo,
                                         xsrc[:, kc * OWN + s * 512:
                                              kc * OWN + (s + 1) * 512],
                                         start=(first and pi == 0),
                                         stop=(last and pi == 2))
                    ins.then_inc(s_kvmm.sem, 1)
                for tt in range(NKC):
                    pst = [ps_tp0, ps_tp1][tt % 2]
                    e.wait_ge(s_kve.sem, 2 * iw + (tt // 8) + 1)
                    if tt >= 2:
                        e.wait_ge(s_tpe.sem, iw * NKC + tt - 1)
                    e.transpose(pst[:, 0:128],
                                kvf[:, (tt % 8) * 128:(tt % 8) * 128 + 128],
                                idn[:]).then_inc(s_tpmm.sem, 1)
            ps = ps_vk0
            e.wait_ge(s_tpe.sem, (grp * 2 + 2) * NKC)
            if grp >= 1:
                e.wait_ge(s_vke.sem, grp)
            e.matmul(ps[0:32, 0:4 * E], zt[:, 0:32], zt[:, 0:4 * E],
                     start=True, stop=False)
            for tt in range(NKC):
                for j in range(4):
                    e.matmul(ps[0:32, E * j:E * j + E],
                             ktt[:, tt * 128 + 32 * j: tt * 128 + 32 * j + 32],
                             vpt[:, tt * 4 * E + j * E: tt * 4 * E + (j + 1) * E],
                             start=False, stop=False)
            e.matmul(ps[0:32, 0:4 * E], zt[:, 0:32], zt[:, 0:4 * E],
                     start=False, stop=True).then_inc(s_vkmm.sem, 1)
            tern.pump_tensor(e, 5)
    s_kvmm.n = NGRP * 4
    s_tpmm.n = NGRP * 2 * NKC
    s_vkmm.n = NGRP

    @block.scalar
    def _(e):
        em = Emit(e)
        for grp in range(NGRP):
            for which in range(2):
                iw = 2 * grp + which
                if grp or which:
                    e.wait_ge(s_tpmm.sem, iw * NKC)
                for th in range(2):
                    e.wait_ge(s_kvmm.sem, 2 * iw + th + 1)
                    if th == 1:
                        e.wait_ge(s_tpmm.sem, iw * NKC + 8)
                    if which == 0:
                        em.act(kvf[:, :], ps_kv[:],
                               ACT.Relu).then_inc(s_kve.sem, 1)
                    else:
                        em.copy(kvf[:, :], ps_kv[:]).then_inc(s_kve.sem, 1)
            tern.pump_scalar(e, em, 5)
    s_kve.n = NGRP * 4

    if debug:
        @block.gpsimd
        def _(e):
            e.wait_ge(s_vke.sem, NGRP)
            e.dma_start(out=dbg["d_bd"][:, :], in_=bd[:]).then_inc(s_out.sem, 16)
            e.dma_start(out=dbg["d_ncol"][:, :], in_=ncol[:]).then_inc(s_out.sem, 16)
        s_out.n += 32

    free(cm_pkv, cm_pt0, cm_pt1, cm_pv0)
    free(cm_xnph, cm_xnpl, cm_wkf, cm_wkh, cm_wkl, cm_kvf, cm_ktt, cm_vpt)

    for tb in range(NTB):
        t0 = tb * TBW
        prev_out_n = s_out.n
        prev_you = (s_you[0].n, s_you[1].n)

        atth, cm_ath = sb(f"atth{tb}", [128, NKC * TBW], BF16)
        attl, cm_atl = sb(f"attl{tb}", [128, NKC * TBW], BF16)
        qsb, cm_qsb = sb(f"qsb{tb}", [128, NKC * TBW], F32)
        wqf, cm_wqf = sb(f"wqf{tb}", [128, 2048], F32)
        wqh = Pair(f"wqh{tb}", [128, 2048], BF16); cm_wqh = wqh
        wql = Pair(f"wql{tb}", [128, 2048], BF16); cm_wql = wql
        ps_q0, cm_pq0 = psum(f"ps_q0_{tb}", [128, TBW])
        ps_q1, cm_pq1 = psum(f"ps_q1_{tb}", [128, TBW])

        nwq0, nwqb0, nwqs0, nqmm0, nqe0 = (s_wq[0].n, s_wqb[0].n, s_wqs.n,
                                           s_qmm.n, s_qe.n)

        def entry_wait(e, tb=tb, prev_out_n=prev_out_n, prev_you=prev_you):
            if tb == 0:
                e.wait_ge(s_vke.sem, NGRP)
                e.wait_ge(s_vkmm.sem, NGRP)
            else:
                e.wait_ge(s_you[0].sem, prev_you[0])
                e.wait_ge(s_you[1].sem, prev_you[1])
                if prev_out_n:
                    e.wait_ge(s_out.sem, prev_out_n)

        @block.sync
        def _(e, nwqs0=nwqs0, nqmm0=nqmm0, entry_wait=entry_wait):
            entry_wait(e)
            for m in range(NKC):
                if m >= 1:
                    e.wait_ge(s_wqs.sem, nwqs0 + m)
                e.dma_start(out=wqf[:].rearrange(
                    "p (kc q) -> p kc q", kc=NKC),
                    in_=wqT[:, m * 128:(m + 1) * 128].rearrange(
                        "(kc p) q -> p kc q", p=128)
                    ).then_inc(s_wq[m % 2].sem, 16)
                if tb == 0 and m % 4 == 3:
                    tern.pump_sync(e, 2)
        s_wq[0].n += NKC * 8
        s_wq[1].n += NKC * 8

        @block.gpsimd
        def _(e, nwqs0=nwqs0, nqmm0=nqmm0, entry_wait=entry_wait):
            entry_wait(e)
            for m in range(NKC):
                if m >= 2:
                    e.wait_ge(s_qmm.sem, nqmm0 + (m - 1) * NKC)
                e.dma_start(out=wqh[m % 2][:].rearrange(
                    "p (kc q) -> p kc q", kc=NKC),
                    in_=wqT[:, m * 128:(m + 1) * 128].rearrange(
                        "(kc p) q -> p kc q", p=128)
                    ).then_inc(s_wqb[m % 2].sem, 16)
                if tb == 0 and m % 4 == 3:
                    tern.pump_gpsimd(e, 2)
        s_wqb[0].n += NKC * 8
        s_wqb[1].n += NKC * 8

        @block.vector
        def _(e, nwq0=nwq0, nwqb0=nwqb0, nqmm0=nqmm0, entry_wait=entry_wait):
            em = Emit(e)
            entry_wait(e)
            for m in range(NKC):
                e.wait_ge(s_wq[m % 2].sem, nwq0 + 16 * (m // 2 + 1))
                e.wait_ge(s_wqb[m % 2].sem, nwqb0 + 16 * (m // 2 + 1))
                if m >= 2:
                    e.wait_ge(s_qmm.sem, nqmm0 + (m - 1) * NKC)
                em.tt(wql[m % 2], wqf[:], wqh[m % 2],
                      ALU.subtract).then_inc(s_wqs.sem, 1)
                if tb == 0 and m % 4 == 3:
                    tern.pump_vector(e, em, 2)
        s_wqs.n += NKC

        @block.tensor
        def _(e, t0=t0, nwqs0=nwqs0, nqe0=nqe0, entry_wait=entry_wait):
            entry_wait(e)
            for i in range(NKC * NKC):
                m, kc = divmod(i, NKC)
                psq = [ps_q0, ps_q1][m % 2]
                if kc == 0 and m >= 2:
                    e.wait_ge(s_qe.sem, nqe0 + m - 1)
                if kc == 0:
                    e.wait_ge(s_wqs.sem, nwqs0 + m + 1)
                b = m % 2
                xh = xnh[:, kc * OWN + t0: kc * OWN + t0 + TBW]
                xl = xnl[:, kc * OWN + t0: kc * OWN + t0 + TBW]
                wh_ = wqh[b][:, kc * 128:(kc + 1) * 128]
                wl_ = wql[b][:, kc * 128:(kc + 1) * 128]
                e.matmul(psq[:], wh_, xh, start=(kc == 0), stop=False)
                e.matmul(psq[:], wl_, xh, start=False, stop=False)
                e.matmul(psq[:], wh_, xl, start=False,
                         stop=(kc == NKC - 1)).then_inc(s_qmm.sem, 1)
                if tb == 0 and kc == NKC - 1 and m % 4 == 3:
                    tern.pump_tensor(e, 2)
        s_qmm.n += NKC * NKC

        @block.scalar
        def _(e, nqmm0=nqmm0):
            em = Emit(e)
            for m in range(NKC):
                e.wait_ge(s_qmm.sem, nqmm0 + NKC * (m + 1))
                psq = [ps_q0, ps_q1][m % 2]
                em.act(qsb[:, m * TBW:(m + 1) * TBW], psq[:],
                       ACT.Relu).then_inc(s_qe.sem, 1)
                if tb == 0 and m % 4 == 3:
                    tern.pump_scalar(e, em, 2)
        s_qe.n += NKC

        if debug and tb == 0:
            @block.gpsimd
            def _(e, nqe0=nqe0):
                e.wait_ge(s_qe.sem, nqe0 + NKC)
                e.dma_start(out=dbg["d_q"][:, :], in_=qsb[:]).then_inc(s_out.sem, 16)
            s_out.n += 16

        nrm, cm_nrm = sb(f"nrm{tb}", [4, TBW], F32)
        nrh, cm_nrh = sb(f"nrh{tb}", [4, TBW], BF16)
        nrl, cm_nrl = sb(f"nrl{tb}", [4, TBW], BF16)
        rbs, cm_rbs = sb(f"rbs{tb}", [128, TBW], F32)
        attf, cm_attf = sb(f"attf{tb}", [128, TBW], F32)
        ps_mn, cm_pmn = psum(f"ps_mn{tb}", [128, TBW])
        ps_nm, cm_pnm = psum(f"ps_nm{tb}", [4, TBW])
        ps_rc, cm_prc = psum(f"ps_rc{tb}", [128, TBW])

        noe0, nov0 = s_oemm.n, s_oev.n

        @block.tensor
        def _(e, noe0=noe0, nov0=nov0, nqe0=nqe0):
            for g in range(NGRP):
                e.wait_ge(s_qe.sem, nqe0 + g + 1)
                if g >= 1:
                    e.wait_ge(s_oev.sem, nov0 + 2 * g)
                q = qsb[:, g * TBW:(g + 1) * TBW]
                e.matmul(ps_nm[0:4, :], ncol[:, g * 4:(g + 1) * 4], q,
                         start=True, stop=True).then_inc(s_oemm.sem, 1)
                e.matmul(ps_mn[:], bd[:, g * 128:(g + 1) * 128], q,
                         start=True, stop=True)
                e.wait_ge(s_oev.sem, nov0 + 2 * g + 1)
                e.matmul(ps_rc[:], ind4[:], nrh[:], start=True, stop=False)
                e.matmul(ps_rc[:], ind4[:], nrl[:], start=False,
                         stop=True).then_inc(s_oemm.sem, 1)
                if tb == 0 and g % 4 == 3:
                    tern.pump_tensor(e, {3: 4, 7: 4, 11: 4, 15: 3}[g])
        s_oemm.n += NGRP * 2

        @block.vector
        def _(e, noe0=noe0):
            em = Emit(e)
            for g in range(NGRP):
                e.wait_ge(s_oemm.sem, noe0 + 2 * g + 1)
                em.ts(nrm[:], ps_nm[:], ATT_EPS, ALU.add)
                em.recip(nrm[:], nrm[:])
                em.copy(nrh[:], nrm[:])
                em.tt(nrl[:], nrm[:], nrh[:], ALU.subtract).then_inc(s_oev.sem, 1)
                e.wait_ge(s_oemm.sem, noe0 + 2 * g + 2)
                em.copy(rbs[:], ps_rc[:])
                em.tt(attf[:], ps_mn[:], rbs[:], ALU.mult)
                em.copy(atth[:, g * TBW:(g + 1) * TBW], attf[:])
                em.tt(attl[:, g * TBW:(g + 1) * TBW], attf[:],
                      atth[:, g * TBW:(g + 1) * TBW],
                      ALU.subtract).then_inc(s_oev.sem, 1)
                if tb == 0 and g % 4 == 3:
                    tern.pump_vector(e, em, {3: 4, 7: 4, 11: 3, 15: 3}[g])
        s_oev.n += NGRP * 2

        if tb == 0:
            @block.scalar
            def _(e):
                em = Emit(e)
                tern.pump_scalar(e, em, 15)

            @block.sync
            def _(e):
                tern.pump_sync(e, 16)

            @block.gpsimd
            def _(e):
                tern.pump_gpsimd(e, 13)
            assert tern.ig == NT_T and tern.iv == NT_T \
                and tern.ipe == NT_T and tern.isc == NT_T \
                and tern.iwb == NT_T, \
                (tern.ig, tern.iv, tern.ipe, tern.isc, tern.iwb)

        if debug and tb == 0:
            @block.gpsimd
            def _(e, nov0=nov0):
                e.wait_ge(s_oev.sem, nov0 + 2 * NGRP)
                e.dma_start(out=dbg["d_atth"][:, :],
                            in_=atth[:]).then_inc(s_out.sem, 16)
            s_out.n += 16

        free(cm_pmn, cm_pnm, cm_prc)
        free(cm_pq0, cm_pq1)
        free(cm_qsb, cm_wqf, cm_wqh, cm_wql, cm_nrm, cm_nrh, cm_nrl,
             cm_rbs, cm_attf)

        nout_dbg = s_out.n
        ffnr, cm_ffnr = sb(f"ffnr{tb}", [128, NKC * TBW], F32)
        gh, cm_gh = sb(f"gh{tb}", [128, (HS // 128) * TBW], BF16)
        gl, cm_gl = sb(f"gl{tb}", [128, (HS // 128) * TBW], BF16)
        w1r = Pair(f"w1r{tb}", [128, 16 * 128], BF16); cm_w1r = w1r
        w2r = Pair(f"w2r{tb}", [128, 2048], BF16); cm_w2r = w2r
        grf, cm_grf = sb(f"grf{tb}", [128, TBW], F32)
        ggf = Pair(f"ggf{tb}", [128, TBW], F32); cm_ggf = ggf
        gg2 = Ring(f"gg2{tb}", [128, TBW], F32, 2); cm_gg2 = gg2
        msv, cm_msv = sb(f"msv{tb}", [1, TBW], F32)
        ps_rs, cm_prs = psum(f"ps_rs{tb}", [128, TBW])
        ps_fc = []
        cm_fcs = []
        for z in range(2):
            t, cm = psum(f"ps_fc{z}_{tb}", [128, TBW])
            ps_fc.append(t)
            cm_fcs.append(cm)
        ps_fp0, cm_pfp0 = psum(f"ps_fp0_{tb}", [128, TBW])
        ps_fp1, cm_pfp1 = psum(f"ps_fp1_{tb}", [128, TBW])

        nw1r0 = s_w1r[0].n
        nfc0 = s_fcmm.n
        nfce0 = s_fce.n
        ngle0 = s_gle.n
        nw2r0 = s_w2r[0].n
        nfp0 = s_fpmm.n
        nfpe0 = s_fpe.n
        nms0 = s_msv.n
        nrsb0 = s_rsb.n

        @block.vector
        def _(e, nout_dbg=nout_dbg):
            em = Emit(e)
            if debug:
                e.wait_ge(s_out.sem, nout_dbg)
            em.memset(ffnr[:], 0.0)
            e.drain()

        @block.sync
        def _(e, nw1r0=nw1r0, nfc0=nfc0, nov0=nov0, nout_dbg=nout_dbg):
            for s4 in s_tw:
                e.wait_ge(s4.sem, (NT_T // 2) * 16)
            e.wait_ge(s_oev.sem, nov0 + 2 * NGRP)
            if debug:
                e.wait_ge(s_out.sem, nout_dbg)
            for m in range(NHC):
                if m >= 2:
                    e.wait_ge(s_fcmm.sem, nfc0 + m - 1)
                e.dma_start(out=w1r[m % 2],
                            in_=w1s[m, :, :, :].rearrange("t p q -> p t q")
                            ).then_inc(s_w1r[m % 2].sem, 16)
        s_w1r[0].n += (NHC // 2) * 16
        s_w1r[1].n += (NHC // 2) * 16

        @block.tensor
        def _(e, t0=t0, nw1r0=nw1r0, nfce0=nfce0, ngle0=ngle0, nw2r0=nw2r0,
              nfpe0=nfpe0, nrsb0=nrsb0, nms0=nms0):
            ip = 0
            for hs in range(NHS):
                for mloc in range(8):
                    m = hs * 8 + mloc
                    e.wait_ge(s_w1r[m % 2].sem, nw1r0 + 16 * (m // 2 + 1))
                    if m >= 2:
                        e.wait_ge(s_fce.sem, nfce0 + m - 1)
                    for kc in range(NKC):
                        xhk = xnh[:, kc * OWN + t0: kc * OWN + t0 + TBW]
                        xlk = xnl[:, kc * OWN + t0: kc * OWN + t0 + TBW]
                        w = w1r[m % 2][:, kc * 128:(kc + 1) * 128]
                        e.matmul(ps_fc[m % 2][:], w, xhk, start=(kc == 0), stop=False)
                        ins = e.matmul(ps_fc[m % 2][:], w, xlk, start=False,
                                       stop=(kc == NKC - 1))
                    ins.then_inc(s_fcmm.sem, 1)
                    if m >= 1:
                        e.wait_ge(s_fce.sem, nfce0 + m)
                        e.matmul(ps_rs[0:1, :], onesf[:], gg2[(m - 1) % 2],
                                 start=(m == 1),
                                 stop=False).then_inc(s_msv.sem, 1)
                e.wait_ge(s_gle.sem, ngle0 + 8 * (hs + 1))
                for mcp in range(8):
                    if hs or mcp:
                        e.wait_ge(s_fpe.sem, nfpe0 + 2 * (hs * 8 + mcp - 1) + 2)
                    iblk = hs * 8 + mcp
                    e.wait_ge(s_w2r[iblk % 2].sem, nw2r0 + 32 * (iblk // 2 + 1))
                    b2 = iblk % 2
                    for khl in range(8):
                        for half2 in range(2):
                            w = w2r[b2][:, khl * 256 + half2 * 128:
                                        khl * 256 + (half2 + 1) * 128]
                            gsl = slice(khl * TBW, (khl + 1) * TBW)
                            psf = [ps_fp0, ps_fp1][half2]
                            e.matmul(psf[:], w, gh[:, gsl],
                                     start=(khl == 0), stop=False)
                            ins = e.matmul(psf[:], w, gl[:, gsl], start=False,
                                           stop=(khl == 7))
                        ins.then_inc(s_fpmm.sem, 1)
                    ip += 1
            e.wait_ge(s_fce.sem, nfce0 + NHC)
            e.matmul(ps_rs[0:1, :], onesf[:], gg2[(NHC - 1) % 2],
                     start=False, stop=True).then_inc(s_msv.sem, 1)
            e.wait_ge(s_rsb.sem, nrsb0 + 3)
            e.matmul(ps_rs[:], ones1[:], rstd_b[0:1, t0:t0 + TBW],
                     start=True, stop=False)
            e.matmul(ps_rs[:], ones1[:], rstd_b[0:1, OWN + t0:OWN + t0 + TBW],
                     start=False, stop=True).then_inc(s_rsb.sem, 1)
        s_fcmm.n += NHC
        s_msv.n += NHC
        s_fpmm.n += NHS * 8 * 8
        s_rsb.n += 1

        @block.scalar
        def _(e, nfc0=nfc0, ngle0=ngle0, nms0=nms0, nfp0=nfp0, nrsb0=nrsb0,
              nov0=nov0, nout_dbg=nout_dbg):
            em = Emit(e)
            e.wait_ge(s_oev.sem, nov0 + 2 * NGRP)
            if debug:
                e.wait_ge(s_out.sem, nout_dbg)
            ip = 0
            for hs in range(NHS):
                for mloc in range(8):
                    m = hs * 8 + mloc
                    e.wait_ge(s_fcmm.sem, nfc0 + m + 1)
                    if m >= 2:
                        e.wait_ge(s_gle.sem, ngle0 + m - 1)
                        e.wait_ge(s_msv.sem, nms0 + m - 1)
                    em.act(grf[:], ps_fc[m % 2][:], ACT.Relu)
                    em.act(ggf[m % 2], grf[:], ACT.Square)
                    em.act(gh[:, mloc * TBW:(mloc + 1) * TBW], grf[:], ACT.Square)
                    em.act(gg2[m % 2], ggf[m % 2],
                           ACT.Square).then_inc(s_fce.sem, 1)
                for mcp in range(8):
                    if ip >= 2:
                        e.wait_ge(s_fpmm.sem, nfp0 + (ip - 1) * 8)
                    for t_ in range(2):
                        e.dma_start(
                            out=w2r[ip % 2][:].rearrange(
                                "p (k t q) -> p k t q", k=8, t=2)[:, :, t_, :],
                            in_=w2s[hs * 8:(hs + 1) * 8, mcp * 2 + t_, :, :]
                            .rearrange("k p q -> p k q")
                            ).then_inc(s_w2r[ip % 2].sem, 16)
                    ip += 1
            e.wait_ge(s_rsb.sem, nrsb0 + 1)
            em.act(msv[:], msv[:], ACT.Sqrt, bias=epst[0:1, 1:2]).then_inc(s_rsb.sem, 1)
        s_fce.n += NHC
        s_w2r[0].n += NHS * 4 * 32
        s_w2r[1].n += NHS * 4 * 32
        s_rsb.n += 1

        @block.vector
        def _(e, t0=t0, nfce0=nfce0, nfp0=nfp0, nfpe0=nfpe0, nms0=nms0,
              nrsb0=nrsb0):
            em = Emit(e)
            for hs in range(NHS):
                for mloc in range(8):
                    m = hs * 8 + mloc
                    e.wait_ge(s_fce.sem, nfce0 + m + 1)
                    em.tt(gl[:, mloc * TBW:(mloc + 1) * TBW], ggf[m % 2],
                          gh[:, mloc * TBW:(mloc + 1) * TBW],
                          ALU.subtract).then_inc(s_gle.sem, 1)
                for mcp in range(8):
                    e.wait_ge(s_fpmm.sem, nfp0 + (hs * 8 + mcp) * 8 + 8)
                    for half2 in range(2):
                        mc = mcp * 2 + half2
                        psf = [ps_fp0, ps_fp1][half2]
                        em.tt(ffnr[:, mc * TBW:(mc + 1) * TBW],
                              ffnr[:, mc * TBW:(mc + 1) * TBW], psf[:],
                              ALU.add).then_inc(s_fpe.sem, 1)
            e.wait_ge(s_msv.sem, nms0 + NHC)
            em.ts(msv[:], ps_rs[0:1, :], INV_HID, ALU.mult).then_inc(s_rsb.sem, 1)
            e.wait_ge(s_rsb.sem, nrsb0 + 2)
            em.recip(msv[:], msv[:])
            em.copy(rstd_b[0:1, t0:t0 + TBW], msv[:])
            em.tt(rstd_b[0:1, OWN + t0:OWN + t0 + TBW], msv[:],
                  rstd_b[0:1, t0:t0 + TBW], ALU.subtract).then_inc(s_rsb.sem, 1)
        s_gle.n += NHC
        s_fpe.n += NHS * 16
        s_rsb.n += 2

        if debug and tb == 0:
            @block.gpsimd
            def _(e, nfpe0=nfpe0):
                e.wait_ge(s_fpe.sem, nfpe0 + NHS * 16)
                e.dma_start(out=dbg["d_ffnr"][:, :],
                            in_=ffnr[:]).then_inc(s_out.sem, 16)
            s_out.n += 16

        free(*cm_fcs, cm_pfp0, cm_pfp1)
        free(cm_gh, cm_gl, cm_w1r, cm_w2r, cm_grf, cm_ggf, cm_gg2, cm_msv)

        wpf, cm_wpf = sb(f"wpf{tb}", [128, 2048], F32)
        wph = Pair(f"wph{tb}", [128, 2048], BF16); cm_wph = wph
        wpl = Pair(f"wpl{tb}", [128, 2048], BF16); cm_wpl = wpl
        xre = Pair(f"xre{tb}", [128, TBW], F32); cm_xre = xre
        fin1, cm_f1 = sb(f"fin1{tb}", [128, TBW], F32)
        outb = Pair(f"outb{tb}", [128, TBW], F32); cm_ob = outb
        ps_p0, cm_pp0b = psum(f"ps_p0_{tb}", [128, TBW])
        ps_p1, cm_pp1b = psum(f"ps_p1_{tb}", [128, TBW])

        nwp0, nwpb0, nwps0, npmm0, nfin0 = (s_wp[0].n, s_wpb[0].n, s_wps.n,
                                            s_pmm.n, s_fin.n)
        nxre0 = s_xre[0].n
        nout0 = s_out.n
        nyou0 = (s_you[0].n, s_you[1].n)

        @block.sync
        def _(e, t0=t0, nwps0=nwps0, npmm0=npmm0, nfin0=nfin0, nfpe0=nfpe0):
            e.wait_ge(s_fpe.sem, nfpe0 + NHS * 16)

            def xre_load(mx):
                if mx >= 2:
                    e.wait_ge(s_fin.sem, nfin0 + mx - 1)
                e.dma_start(out=xre[mx % 2],
                            in_=xT[mx * 128:(mx + 1) * 128, t0:t0 + TBW]
                            ).then_inc(s_xre[mx % 2].sem, 16)

            for m in range(NKC):
                if m >= 2:
                    xre_load(m - 2)
                if m >= 1:
                    e.wait_ge(s_wps.sem, nwps0 + m)
                e.dma_start(out=wpf[:].rearrange(
                    "p (kc q) -> p kc q", kc=NKC),
                    in_=wpT[:, m * 128:(m + 1) * 128].rearrange(
                        "(kc p) q -> p kc q", p=128)
                    ).then_inc(s_wp[m % 2].sem, 16)
            xre_load(NKC - 2)
            xre_load(NKC - 1)
        s_wp[0].n += NKC * 8
        s_wp[1].n += NKC * 8
        s_xre[0].n += NKC * 8
        s_xre[1].n += NKC * 8

        @block.gpsimd
        def _(e, t0=t0, nwps0=nwps0, npmm0=npmm0, nfpe0=nfpe0, nfin0=nfin0):
            e.wait_ge(s_fpe.sem, nfpe0 + NHS * 16)

            def y_dma(my):
                e.wait_ge(s_fin.sem, nfin0 + my + 1)
                e.dma_start(out=y[my * 128:(my + 1) * 128, t0:t0 + TBW],
                            in_=outb[my % 2][:]).then_inc(s_you[my % 2].sem, 16)

            for m in range(NKC):
                if m >= 2:
                    y_dma(m - 2)
                    e.wait_ge(s_pmm.sem, npmm0 + (m - 1) * NKC)
                e.dma_start(out=wph[m % 2][:].rearrange(
                    "p (kc q) -> p kc q", kc=NKC),
                    in_=wpT[:, m * 128:(m + 1) * 128].rearrange(
                        "(kc p) q -> p kc q", p=128)
                    ).then_inc(s_wpb[m % 2].sem, 16)
            y_dma(NKC - 2)
            y_dma(NKC - 1)
        s_wpb[0].n += NKC * 8
        s_wpb[1].n += NKC * 8
        s_you[0].n += NKC * 8
        s_you[1].n += NKC * 8

        @block.vector
        def _(e, t0=t0, nwp0=nwp0, nwpb0=nwpb0, npmm0=npmm0, nfin0=nfin0,
              nout0=nout0, nxre0=nxre0, nyou0=nyou0):
            em = Emit(e)

            def final(m):
                e.wait_ge(s_pmm.sem, npmm0 + NKC * (m + 1))
                e.wait_ge(s_xre[m % 2].sem, nxre0 + 16 * (m // 2 + 1))
                if m >= 2:
                    e.wait_ge(s_you[m % 2].sem, nyou0[m % 2] + 16 * (m // 2))
                psp = [ps_p0, ps_p1][m % 2]
                em.tt(fin1[:], ffnr[:, m * TBW:(m + 1) * TBW], ps_rs[:], ALU.mult)
                em.tt(fin1[:], fin1[:], psp[:], ALU.add)
                em.stt(outb[m % 2], fin1[:], bpt[:, m:m + 1], xre[m % 2],
                       ALU.add, ALU.add).then_inc(s_fin.sem, 1)

            for m in range(NKC):
                if m >= 2:
                    final(m - 2)
                e.wait_ge(s_wp[m % 2].sem, nwp0 + 16 * (m // 2 + 1))
                e.wait_ge(s_wpb[m % 2].sem, nwpb0 + 16 * (m // 2 + 1))
                if m >= 2:
                    e.wait_ge(s_pmm.sem, npmm0 + (m - 1) * NKC)
                em.tt(wpl[m % 2], wpf[:], wph[m % 2],
                      ALU.subtract).then_inc(s_wps.sem, 1)
            for m in range(NKC - 2, NKC):
                final(m)
        s_wps.n += NKC
        s_fin.n += NKC

        @block.tensor
        def _(e, nwps0=nwps0, nfin0=nfin0, nrsb0=nrsb0):
            e.wait_ge(s_rsb.sem, nrsb0 + 4)
            for i in range(NKC * NKC):
                m, kc = divmod(i, NKC)
                psp = [ps_p0, ps_p1][m % 2]
                if kc == 0 and m >= 2:
                    e.wait_ge(s_fin.sem, nfin0 + m - 1)
                if kc == 0:
                    e.wait_ge(s_wps.sem, nwps0 + m + 1)
                b = m % 2
                ah = atth[:, kc * TBW:(kc + 1) * TBW]
                al = attl[:, kc * TBW:(kc + 1) * TBW]
                wh_ = wph[b][:, kc * 128:(kc + 1) * 128]
                wl_ = wpl[b][:, kc * 128:(kc + 1) * 128]
                e.matmul(psp[:], wh_, ah, start=(kc == 0), stop=False)
                e.matmul(psp[:], wl_, ah, start=False, stop=False)
                e.matmul(psp[:], wh_, al, start=False,
                         stop=(kc == NKC - 1)).then_inc(s_pmm.sem, 1)
        s_pmm.n += NKC * NKC



        free(cm_pp0b, cm_pp1b)
        free(cm_prs)
        free(cm_ath, cm_atl, cm_ffnr, cm_wpf, cm_wph, cm_wpl, cm_xre,
             cm_f1, cm_ob)

    @block.gpsimd
    def _(e):
        e.wait_ge(s_you[0].sem, s_you[0].n)
        e.wait_ge(s_you[1].sem, s_you[1].n)
        if s_out.n:
            e.wait_ge(s_out.sem, s_out.n)

    block_cm.__exit__(None, None, None)
    return nc, dbg


_CONSTS = None


def _consts():
    global _CONSTS
    if _CONSTS is None:
        sel = np.zeros((128, 2), ml_dtypes.bfloat16)
        for p in range(128):
            sel[p, p // 64] = ml_dtypes.bfloat16(1.0 / 64.0)
        ind2 = np.zeros((2, 128), ml_dtypes.bfloat16)
        for p in range(128):
            ind2[p // 64, p] = ml_dtypes.bfloat16(1.0)
        ind4 = np.zeros((4, 128), ml_dtypes.bfloat16)
        for p in range(128):
            ind4[p // 32, p] = ml_dtypes.bfloat16(1.0)
        ones1 = np.ones((1, 128), ml_dtypes.bfloat16)
        onesf = np.ones((128, 1), np.float32)
        idn = np.eye(128, dtype=np.float32)
        idnb = np.eye(128, dtype=np.float32).astype(ml_dtypes.bfloat16)
        eps = np.array([[LN_EPS, RMS_EPS]], np.float32)
        _CONSTS = dict(seld=sel, ind2d=ind2, ind4d=ind4, ones1d=ones1,
                       onesfd=onesf, idnd=idn, idnbd=idnb, epsd=eps)
    return _CONSTS


def make_in_maps(x, wq, wp, wf, w2, bp):
    consts = _consts()
    in_maps = []
    for core in range(8):
        b, half = divmod(core, 2)
        xt = np.asarray(x[b].T)
        if half == 0:
            xcore = np.ascontiguousarray(xt)
        else:
            xcore = np.ascontiguousarray(
                np.concatenate([xt[:, OWN:], xt[:, :OWN]], axis=1))
        im = dict(xT=xcore, wqT=wq, wpT=wp, wfn=wf, w2n=w2, bpd=bp)
        im.update(consts)
        in_maps.append(im)
    return in_maps


def kernel(L0_attn_x, L0_attn_w_qkv, L0_attn_w_proj, L0_attn_b_proj,
           L0_ffn_W_fc, L0_ffn_W_proj):
    x = np.asarray(L0_attn_x, dtype=np.float32)
    wq = np.ascontiguousarray(np.asarray(L0_attn_w_qkv, np.float32).T)
    wp = np.ascontiguousarray(np.asarray(L0_attn_w_proj, np.float32).T)
    wf = np.ascontiguousarray(np.asarray(L0_ffn_W_fc, np.float32))
    w2 = np.ascontiguousarray(np.asarray(L0_ffn_W_proj, np.float32))
    bp = np.ascontiguousarray(
        np.asarray(L0_attn_b_proj, np.float32).reshape(NKC, 128).T)

    nc, _ = build_program(debug=False)
    in_maps = make_in_maps(x, wq, wp, wf, w2, bp)
    res = run_bass_kernel_spmd(nc, in_maps, core_ids=list(range(8)))
    out = np.empty((4, TOK, C), np.float32)
    for core in range(8):
        b, half = divmod(core, 2)
        yc = res.results[core]["y"]
        out[b, half * OWN:(half + 1) * OWN, :] = yc.T
    return out
